# version 17
# baseline (speedup 1.0000x reference)
import numpy as np
import ml_dtypes
import concourse.bass as bass
import concourse.mybir as mybir
from contextlib import ExitStack

F32 = mybir.dt.float32
BF16 = mybir.dt.bfloat16
AF = mybir.ActivationFunctionType
ALU = mybir.AluOpType
AX = mybir.AxisListType

ENGS = ("sync", "vector", "tensor", "scalar", "gpsimd")
ESEM = {"tensor": "s_t", "vector": "s_v", "scalar": "s_a", "gpsimd": "s_g"}


def cdiv(a, b):
    return (a + b - 1) // b


class Prog:
    def __init__(self, nc):
        self.nc = nc
        self.es = ExitStack()
        self.ops = {e: [] for e in ENGS}
        self.sems = {}
        self.cnt = {}
        self.waited = {e: {} for e in ENGS}
        self.dirty = {e: False for e in ENGS}
        self.epoch = ""

    def esem(self, eng):
        return ESEM[eng] + self.epoch

    def sem(self, name):
        if name not in self.sems:
            self.sems[name] = self.es.enter_context(self.nc.semaphore(name))
            self.cnt[name] = 0
        return self.sems[name]

    def raw(self, eng, fn):
        self.ops[eng].append(fn)
        self.dirty[eng] = True

    def drain(self, eng):
        if self.dirty[eng]:
            self.ops[eng].append(lambda e: e.drain())
            self.dirty[eng] = False

    def wait(self, eng, semname, val):
        if val <= 0:
            return
        w = self.waited[eng]
        if w.get(semname, 0) >= val:
            return
        w[semname] = val
        s = self.sem(semname)
        self.ops[eng].append(lambda e, s=s, v=val: e.wait_ge(s, v))

    def dep(self, eng, marker):
        if marker is None:
            return
        semname, val, peng = marker
        if peng == eng:
            self.drain(eng)
        else:
            self.wait(eng, semname, val)

    def deps(self, eng, markers):
        for m in markers:
            self.dep(eng, m)

    def inc(self, eng, fn, n=1, semname=None):
        semname = semname or (ESEM[eng] + self.epoch if eng in ESEM else "s_x")
        s = self.sem(semname)
        self.cnt[semname] += n
        def f(e, fn=fn, s=s, n=n):
            fn(e).then_inc(s, n)
        self.ops[eng].append(f)
        self.dirty[eng] = True
        return (semname, self.cnt[semname], eng)

    def mark(self, eng):
        semname = ESEM[eng] + self.epoch
        self.sem(semname)
        return (semname, self.cnt.get(semname, 0), eng)

    def dma(self, qeng, dsem, out, in_, deps=(), slow=False):
        self.deps(qeng, deps)
        if not hasattr(self, "dsems"):
            self.dsems = {}
        self.dsems[dsem] = qeng
        m = self.inc(qeng, lambda e, o=out, i=in_, sl=slow: e.dma_start(
            out=o, in_=i, allow_slow_non_contiguous=sl), 16, semname=dsem)
        return (m[0], m[1], "DMA:" + qeng)

    def finish(self, markers):
        for m in markers:
            if m is not None:
                self.wait("gpsimd", m[0], m[1])

    def build(self):
        with self.es, self.nc.Block() as block:
            for eng in ENGS:
                ops = self.ops[eng]
                getattr(block, eng)(lambda e, ops=ops: [fn(e) for fn in ops])


class BankPool:
    def __init__(self, banks):
        self.banks = [{"ap": b, "free": None} for b in banks]
        self.i = 0

    def next(self):
        b = self.banks[self.i % len(self.banks)]
        self.i += 1
        return b


class Arena:

    def __init__(self, nc, es, name, nbytes):
        nbytes = (nbytes + 31) // 32 * 32
        self.t = es.enter_context(nc.sbuf_tensor(name, [128, nbytes], mybir.dt.uint8))
        self.nbytes = nbytes

    def view(self, off, dtype, shape):
        bsz = 4 if dtype == F32 else 2
        n = int(np.prod(shape))
        assert off % 32 == 0 and off + n * bsz <= self.nbytes, (off, n, bsz, self.nbytes)
        ap = self.t.ap().bitcast(dtype)[:, off // bsz: off // bsz + n]
        if len(shape) == 2:
            ap = ap.rearrange("p (a b) -> p a b", a=shape[0])
        elif len(shape) == 3:
            ap = ap.rearrange("p (a b c) -> p a b c", a=shape[0], b=shape[1])
        return ap


def wslice_ap(w_hbm, r1, c0, c1):
    return w_hbm[0:r1, c0:c1].rearrange("(k p) n -> p k n", p=128)


class WStream:

    def __init__(self, p, wb, nslots, chunks, qeng="gpsimd", depth=None, base="dw",
                 init_free=None):
        self.p = p
        self.wb = wb
        self.S = nslots
        self.chunks = chunks
        self.base = base
        self.qeng = qeng
        self.free = [list(init_free) if init_free else None] * nslots
        self.loaded = [None] * len(chunks)
        self.n_issued = 0
        depth = nslots if depth is None else depth
        for _ in range(min(depth, len(chunks))):
            self._issue()

    def _issue(self):
        i = self.n_issued
        hbm, kt, nsz = self.chunks[i]
        sl = i % self.S
        dst = self.wb[:, sl, 0:kt, 0:nsz] if self.S > 1 else self.wb[:, 0, 0:kt, 0:nsz]
        self.loaded[i] = self.p.dma(self.qeng, f"{self.base}{sl}", dst, hbm,
                                    deps=_mlist(self.free[sl]))
        self.n_issued += 1

    def slot(self, i):
        return i % self.S

    def use(self, i):
        return self.loaded[i]

    def done(self, i, marker):
        self.free[i % self.S] = marker
        if self.n_issued < len(self.chunks):
            self._issue()



def emit_rstd(p, st, mean_off, var_off, scratch, out_r, out_negmr, eps):
    p.drain("vector")
    mv = p.inc("vector", lambda e: e.tensor_scalar(
        st[:, scratch:scratch + 1], st[:, var_off:var_off + 1], eps, None, ALU.add))
    p.dep("scalar", mv)
    sq = p.inc("scalar", lambda e: e.activation(
        st[:, scratch + 1:scratch + 2], st[:, scratch:scratch + 1], AF.Sqrt))
    p.dep("vector", sq)
    p.raw("vector", lambda e: e.reciprocal(
        st[:, out_r:out_r + 1], st[:, scratch + 1:scratch + 2]))
    p.drain("vector")
    rm = p.inc("vector", lambda e: e.tensor_scalar(
        st[:, out_negmr:out_negmr + 1], st[:, mean_off:mean_off + 1],
        st[:, out_r:out_r + 1], -1.0, ALU.mult, ALU.mult))
    return rm


def emit_ln_tile(p, cfg, bufs, pools, src_ap, src_marker, dst_feat, m, lt_sl,
                 reader_free=None):
    C = cfg["C"]
    CT = C // 128
    NCH = C // 512
    lt, st, idb = bufs["ln_tok"], bufs["stat"], bufs["idb"]
    pt = pools["pt"]
    sb = 64 * (m % 2)
    p.dep("vector", src_marker)
    for c in range(NCH):
        if c:
            p.drain("vector")
        p.raw("vector", lambda e, c=c, sb=sb: e.bn_stats(
            st[:, sb + 6 * c:sb + 6 * c + 6], src_ap[:, 512 * c:512 * (c + 1)]))
    p.drain("vector")
    p.raw("vector", lambda e, sb=sb: e.bn_aggr(
        st[:, sb + 24:sb + 26], st[:, sb:sb + 6 * NCH]))
    rm = emit_rstd(p, st, mean_off=sb + 24, var_off=sb + 25, scratch=sb + 26,
                   out_r=sb + 28, out_negmr=sb + 29, eps=1e-5)
    p.dep("scalar", rm)
    p.dep("scalar", src_marker)
    p.dep("scalar", bufs["lt_free"][lt_sl])
    ap = p.inc("scalar", lambda e, sb=sb: e.activation(
        lt[:, lt_sl, :], src_ap[:, :], AF.Identity,
        bias=st[:, sb + 29:sb + 30], scale=st[:, sb + 28:sb + 29]))
    last = None
    for g in range(cdiv(CT, 8)):
        bank = pools["pt"].next()
        n8 = min(8, CT - 8 * g)
        p.dep("tensor", ap)
        p.dep("tensor", bank["free"])
        for j in range(n8):
            c = 8 * g + j
            fn = lambda e, c=c, j=j, b=bank["ap"]: e.transpose(
                b[:, 128 * j:128 * (j + 1)], lt[:, lt_sl, 128 * c:128 * (c + 1)], idb[:, :])
            if j < n8 - 1:
                p.raw("tensor", fn)
            else:
                tr = p.inc("tensor", fn)
        bufs["lt_free"][lt_sl] = tr
        p.dep("scalar", tr)
        if reader_free is not None:
            p.dep("scalar", reader_free)
        p.drain("scalar")
        ev = p.inc("scalar", lambda e, g=g, n8=n8, m=m, b=bank["ap"]: e.activation(
            dst_feat[:, 8 * g:8 * g + n8, 128 * m:128 * (m + 1)],
            b[:, 0:128 * n8].rearrange("p (j q) -> p j q", j=n8), AF.Identity))
        bank["free"] = ev
        last = ev
    return last, ap


def emit_ln_to_feat(p, cfg, bufs, pools, x_src_fn, dst_feat, reader_free=None):
    T = cfg["T"]
    TT = T // 128
    xb = bufs["xbuf"]
    XB = xb.shape[1]
    marks = []
    for m in range(TT):
        sl = m % XB
        dep0 = bufs["xb_free"][sl]
        dep0 = dep0 if isinstance(dep0, list) else ([dep0] if dep0 else [])
        ld = p.dma("sync", f"dx{sl}", xb[:, sl, :], x_src_fn(m), deps=dep0)
        last, ap = emit_ln_tile(p, cfg, bufs, pools, xb[:, sl, :], ld, dst_feat,
                                m, lt_sl=m % 2, reader_free=reader_free)
        bufs["xb_free"][sl] = [ap]
        marks.append(last)
    return marks


def _mlist(x):
    return x if isinstance(x, list) else ([x] if x else [])


def emit_residual_tail(p, cfg, bufs, pools, ybuf, y_dep, x_src_fn, x_dst_fn,
                       out_markers, pre_scale=None, ln_dst=None,
                       ln_reader_free=None, start_dep=None):
    T, C = cfg["T"], cfg["C"]
    TT = T // 128
    NCH = C // 512
    xb, xo, st = bufs["xbuf"], bufs["xout"], bufs["stat"]
    XS = xo.shape[1]
    XB = xb.shape[1]
    ln_last = None
    ln_marks = []
    add_marks = []
    p.dep("vector", start_dep)
    for m in range(TT):
        sl = m % XB
        xsl = m % XS
        sb = 64 * (m % 2)
        ld = p.dma("sync", f"dx{sl}", xb[:, sl, :], x_src_fn(m),
                   deps=_mlist(bufs["xb_free"][sl]))
        p.dep("vector", y_dep)
        ps_m = pre_scale(p, m) if pre_scale is not None else None
        p.dep("vector", ps_m)
        for c in range(NCH):
            if c:
                p.drain("vector")
            p.raw("vector", lambda e, m=m, c=c, sb=sb: e.bn_stats(
                st[:, sb + 6 * c:sb + 6 * c + 6], ybuf[:, m, 512 * c:512 * (c + 1)]))
        p.drain("vector")
        p.raw("vector", lambda e, sb=sb: e.bn_aggr(
            st[:, sb + 24:sb + 26], st[:, sb:sb + 6 * NCH]))
        rm = emit_rstd(p, st, mean_off=sb + 24, var_off=sb + 25, scratch=sb + 26,
                       out_r=sb + 28, out_negmr=sb + 29, eps=1e-5)
        p.dep("vector", rm)
        p.deps("vector", _mlist(bufs["xo_free"][xsl]) + [ld])
        xw = p.inc("vector", lambda e, m=m, sl=sl, xsl=xsl, sb=sb: e.tensor_scalar(
            xo[:, xsl, :], ybuf[:, m, :], st[:, sb + 28:sb + 29],
            st[:, sb + 29:sb + 30], ALU.mult, ALU.add))
        p.deps("gpsimd", [xw, ld])
        add = p.inc("gpsimd", lambda e, sl=sl, xsl=xsl: e.tensor_add(
            xo[:, xsl, :], xo[:, xsl, :], xb[:, sl, :]))
        bufs["xb_free"][sl] = add
        add_marks.append(add)
        stm = p.dma("scalar", f"dxo{xsl}", x_dst_fn(m), xo[:, xsl, :], deps=[add])
        out_markers.append(stm)
        bufs["xo_free"][xsl] = [stm]
        if ln_dst is not None:
            ln_last, ap = emit_ln_tile(p, cfg, bufs, pools, xo[:, xsl, :], add,
                                       ln_dst, m, lt_sl=m % 2,
                                       reader_free=ln_reader_free if m == 0 else None)
            bufs["xo_free"][xsl].append(ap)
            ln_marks.append(ln_last)
    return (ln_marks if ln_dst is not None else None), add_marks



def emit_linear_tok(p, cfg, pools, lhs_feat, lhs_dep, ws, nsizes, evac):
    T, C = cfg["T"], cfg["C"]
    TT, CT = T // 128, C // 128
    pb = pools["pb"]
    per_m = isinstance(lhs_dep, list)
    for ci in range(len(nsizes)):
        nsz = nsizes[ci]
        sl = ws.slot(ci)
        p.dep("tensor", ws.use(ci))
        if not per_m:
            p.dep("tensor", lhs_dep)
        for m in range(TT):
            bank = pb.next()
            p.dep("tensor", bank["free"])
            if per_m:
                p.dep("tensor", lhs_dep[m])
            for k in range(CT):
                wap = ws.wb[:, sl, k, 0:nsz]
                fn = lambda e, k=k, m=m, w=wap, nsz=nsz, b=bank["ap"]: e.matmul(
                    b[:, 0:nsz], lhs_feat[:, k, 128 * m:128 * (m + 1)], w,
                    start=(k == 0), stop=(k == CT - 1))
                if k < CT - 1:
                    p.raw("tensor", fn)
                else:
                    mm = p.inc("tensor", fn)
            bank["free"] = evac(p, m, ci, bank, mm)
        ws.done(ci, p.mark("tensor"))



def emit_attn_l0(p, cfg, bufs, pools, io, ln_dep, winit=None):
    T, C = cfg["T"], cfg["C"]
    TT, CT = T // 128, C // 128
    NKV, HD = cfg["NKV0"], cfg["HD0"]
    G = T // 512
    TC = T // 512
    SC = 1.0 / np.sqrt(HD)
    DKT = HD // 128
    lf = bufs["ln_feat"]
    qb, kb, vb = bufs["qbuf"], bufs["kbuf"], bufs["vbuf"]
    tb, ab, a2, at = bufs["tbuf"], bufs["abuf"], bufs["a2buf"], bufs["atbuf"]
    tmp, osp = bufs["tmp"], bufs["osp"]
    cosb, sinb, mneg, st, idb = bufs["cosb"], bufs["sinb"], bufs["mneg"], bufs["stat"], bufs["idb"]
    pb, pt, sq = pools["pb"], pools["pt"], pools["sq"]
    o_scr = io["o_scr"]
    Wq, Wk, Wv = io["Wq"], io["Wk"], io["Wv"]

    wi = [winit] if winit else None
    wq_s = WStream(p, bufs["wqb"], 1,
                   [(wslice_ap(Wq, C, HD * qh, HD * (qh + 1)), CT, HD) for qh in range(2 * NKV)],
                   base="dwq", init_free=wi)
    wk_s = WStream(p, bufs["wkb"], 1,
                   [(wslice_ap(Wk, C, HD * h, HD * (h + 1)), CT, HD) for h in range(NKV)],
                   base="dwk", init_free=wi)
    wv_s = WStream(p, bufs["wvb"], 1,
                   [(wslice_ap(Wv, C, HD * h, HD * (h + 1)), CT, HD) for h in range(NKV)],
                   base="dwv", init_free=wi)

    SKEW = 3
    qkv_war = None
    ab_free = [None, None]
    a2_free = [None, None]
    at_free = None
    osp_free = [None, None]
    tmp_free = None
    spill_marks = []

    def rope_proj(w_s, ci, dst, dst_base, war_dep):
        nonlocal tmp_free
        sl = w_s.slot(ci)
        wb = bufs["wqb"] if w_s is wq_s else bufs["wkb"]
        p.dep("tensor", w_s.use(ci))
        for tc in range(TC):
            p.dep("tensor", ln_dep[4 * tc + 3])
            bA = pb.next()
            bB = pb.next()
            p.deps("tensor", [bA["free"], bB["free"]])
            for half, bank in ((0, bA), (1, bB)):
                for k in range(CT):
                    fn = lambda e, k=k, half=half, tc=tc, b=bank["ap"], wb=wb, sl=sl: e.matmul(
                        b[:, 0:512], wb[:, sl, k, 128 * half:128 * (half + 1)],
                        lf[:, k, 512 * tc:512 * (tc + 1)],
                        start=(k == 0), stop=(k == CT - 1))
                    if k < CT - 1:
                        p.raw("tensor", fn)
                    else:
                        mm = p.inc("tensor", fn)
            p.dep("vector", mm)
            p.dep("vector", tmp_free)
            srcs = ((bA, cosb), (bB, sinb), (bB, cosb), (bA, sinb))
            for ti, (src, trig) in enumerate(srcs[:3]):
                p.raw("vector", lambda e, ti=ti, s=src["ap"], t=trig, tc=tc: e.tensor_mul(
                    tmp[:, ti, :], s[:, 0:512], t[:, 512 * tc:512 * (tc + 1)]))
            vm = p.inc("vector", lambda e, s=srcs[3][0]["ap"], t=srcs[3][1], tc=tc:
                       e.tensor_mul(tmp[:, 3, :], s[:, 0:512], t[:, 512 * tc:512 * (tc + 1)]))
            bA["free"] = vm
            bB["free"] = vm
            p.dep("gpsimd", vm)
            p.dep("gpsimd", war_dep)
            p.raw("gpsimd", lambda e, tc=tc, dst=dst, db=dst_base: e.tensor_sub(
                dst[:, db, 512 * tc:512 * (tc + 1)], tmp[:, 0, :], tmp[:, 1, :]))
            p.drain("gpsimd")
            gm = p.inc("gpsimd", lambda e, tc=tc, dst=dst, db=dst_base: e.tensor_add(
                dst[:, db + 1, 512 * tc:512 * (tc + 1)], tmp[:, 2, :], tmp[:, 3, :]))
            tmp_free = gm
        return p.mark("gpsimd")

    for h in range(NKV):
        q_ready = []
        for pair in range(2):
            q_ready.append(rope_proj(wq_s, 2 * h + pair, qb, 2 * pair, qkv_war))
            wq_s.done(2 * h + pair, p.mark("tensor"))
        k_ready = rope_proj(wk_s, h, kb, 0, qkv_war)
        wk_s.done(h, p.mark("tensor"))
        slv = wv_s.slot(h)
        p.dep("tensor", wv_s.use(h))
        v_ev = None
        for m in range(TT):
            bank = pb.next()
            p.deps("tensor", [bank["free"], ln_dep[m]])
            wvb = bufs["wvb"]
            for k in range(CT):
                fn = lambda e, k=k, m=m, slv=slv, b=bank["ap"], wvb=wvb: e.matmul(
                    b[:, 0:HD], lf[:, k, 128 * m:128 * (m + 1)],
                    wvb[:, slv, k, 0:HD], start=(k == 0), stop=(k == CT - 1))
                if k < CT - 1:
                    p.raw("tensor", fn)
                else:
                    mm = p.inc("tensor", fn)
            p.dep("scalar", mm)
            p.dep("scalar", qkv_war)
            v_ev = p.inc("scalar", lambda e, m=m, b=bank["ap"]: e.activation(
                vb[:, m, 0:HD], b[:, 0:HD], AF.Identity))
            bank["free"] = v_ev
        v_ready = v_ev
        wv_s.done(h, p.mark("tensor"))

        units = [(ql, g, il) for ql in range(2) for g in range(G)
                 for il in range(4)]
        mmv = {}
        scv = {}
        evv = {}

        def emit_z(g):
            p.dep("vector", at_free)
            for il in range(4):
                for jj in range(4 * g + il + 1, 4 * (g + 1)):
                    p.inc("vector", lambda e, jj=jj, il=il: e.memset(
                        at[:, jj, 128 * il:128 * (il + 1)], 0.0))

        def emit_sc(u):
            ql, g, il = units[u]
            i = 4 * g + il
            K = 128 * (i + 1)
            p.deps("tensor", [sq["free"], q_ready[ql], k_ready])
            for kc in range(cdiv(K, 512)):
                ksz = min(512, K - 512 * kc)
                for d in range(DKT):
                    fn = lambda e, d=d, i=i, kc=kc, ksz=ksz, ql=ql: e.matmul(
                        sq["ap"][:, 512 * kc:512 * kc + ksz],
                        qb[:, DKT * ql + d, 128 * i:128 * (i + 1)],
                        kb[:, d, 512 * kc:512 * kc + ksz],
                        start=(d == 0), stop=(d == DKT - 1))
                    if kc == cdiv(K, 512) - 1 and d == DKT - 1:
                        mmv[u] = p.inc("tensor", fn)
                    else:
                        p.raw("tensor", fn)

        def emit_chain(u):
            ql, g, il = units[u]
            i = 4 * g + il
            K = 128 * (i + 1)
            par = u % 2
            p.dep("scalar", mmv[u])
            p.drain("scalar")
            ev_t = p.inc("scalar", lambda e, K=K: e.activation(
                tb[:, 0:K], sq["ap"][:, 0:K], AF.Tanh, scale=SC / 50.0))
            sq["free"] = ev_t
            p.dep("vector", ev_t)
            mk_ = p.inc("vector", lambda e, i=i: e.tensor_add(
                tb[:, 128 * i:128 * (i + 1)], tb[:, 128 * i:128 * (i + 1)], mneg[:, :]))
            p.dep("scalar", mk_)
            p.dep("scalar", ab_free[par])
            p.drain("scalar")
            ex = p.inc("scalar", lambda e, K=K, par=par: e.activation(
                ab[:, par, 0:K], tb[:, 0:K], AF.Exp, scale=50.0,
                accum_out=st[:, 40 + par:41 + par]))
            p.dep("vector", ex)
            p.dep("vector", ab_free[par])
            p.drain("vector")
            rz = p.inc("vector", lambda e, par=par: e.reciprocal(
                st[:, 42 + par:43 + par], st[:, 40 + par:41 + par]))
            p.deps("gpsimd", [rz, a2_free[par]])
            scm = p.inc("gpsimd", lambda e, K=K, par=par: e.tensor_scalar(
                a2[:, par, 0:K], ab[:, par, 0:K], st[:, 42 + par:43 + par], 1.0,
                ALU.mult, ALU.mult))
            ab_free[par] = scm
            scv[u] = scm

        def emit_tr(u):
            ql, g, il = units[u]
            i = 4 * g + il
            par = u % 2
            for tg in range(cdiv(i + 1, 8)):
                bank = pt.next()
                n8 = min(8, i + 1 - 8 * tg)
                p.deps("tensor", [scv[u], bank["free"]])
                for j in range(n8):
                    jt = 8 * tg + j
                    fn = lambda e, jt=jt, j=j, par=par, b=bank["ap"]: e.transpose(
                        b[:, 128 * j:128 * (j + 1)], a2[:, par, 128 * jt:128 * (jt + 1)],
                        idb[:, :])
                    if j < n8 - 1:
                        p.raw("tensor", fn)
                    else:
                        tr = p.inc("tensor", fn)
                a2_free[par] = tr
                p.dep("vector", tr)
                ev = p.inc("vector",
                           lambda e, tg=tg, n8=n8, il=il, b=bank["ap"]: e.tensor_copy(
                               at[:, 8 * tg:8 * tg + n8, 128 * il:128 * (il + 1)],
                               b[:, 0:128 * n8].rearrange("p (j q) -> p j q", j=n8)))
                bank["free"] = ev
                evv[u] = ev

        def emit_av(u):
            ql, g, il = units[u]
            qh = 2 * h + ql
            KJ = 4 * (g + 1)
            for dm in range(DKT):
                bank = pb.next()
                p.deps("tensor", [bank["free"], evv[u], v_ready])
                for j in range(KJ):
                    fn = lambda e, j=j, dm=dm, b=bank["ap"]: e.matmul(
                        b[:, 0:512], vb[:, j, 128 * dm:128 * (dm + 1)], at[:, j, 0:512],
                        start=(j == 0), stop=(j == KJ - 1))
                    if j < KJ - 1:
                        p.raw("tensor", fn)
                    else:
                        mm = p.inc("tensor", fn)
                p.dep("scalar", mm)
                p.dep("scalar", osp_free[dm])
                ev = p.inc("scalar", lambda e, dm=dm, b=bank["ap"]: e.activation(
                    osp[:, dm, :], b[:, 0:512], AF.Identity))
                bank["free"] = ev
                stm = p.dma("scalar", f"dsp{dm}", o_scr[g, qh * DKT + dm, :, :],
                            osp[:, dm, :], deps=[ev])
                osp_free[dm] = stm

        av_pend = []

        def flush_av():
            nonlocal at_free
            while av_pend:
                t = av_pend.pop(0)
                emit_av(t)
                at_free = p.mark("tensor")
                nxt = t + 1
                if nxt < len(units):
                    emit_z(units[nxt][1])

        def retire(t):
            emit_tr(t)
            if units[t][2] == 3:
                av_pend.append(t)

        emit_z(0)
        pend = []
        for u in range(len(units)):
            emit_sc(u)
            flush_av()
            if len(pend) == 2:
                retire(pend.pop(0))
            emit_chain(u)
            pend.append(u)
        for t in pend:
            retire(t)
            flush_av()
        qkv_war = p.mark("tensor")
    spill_marks = [osp_free[0], osp_free[1], p.mark("tensor"), p.mark("vector"),
                   p.mark("scalar"), p.mark("gpsimd")]
    return spill_marks



def emit_wo_l0(p, cfg, bufs, pools, io, ybuf, KO=None, bias_cols=None,
               first_deps=None):
    T, C = cfg["T"], cfg["C"]
    TT, CT = T // 128, C // 128
    G = T // 512
    KO = (cfg["NH0"] * cfg["HD0"]) // 128 if KO is None else KO
    NCH = C // 512
    Wo = io["Wo"]
    o_scr = io["o_scr"]
    og, yf, idb = bufs["og"], bufs["yf"], bufs["idb"]
    wob = bufs["wob"]
    pb, pt = pools["pb"], pools["pt"]
    wo_s = WStream(p, wob, 2,
                   [(wslice_ap(Wo, 128 * KO, 512 * cn, 512 * (cn + 1)), KO, 512)
                    for _ in range(G) for cn in range(NCH)], base="dwo",
                   init_free=first_deps)
    og_free = list(first_deps) if first_deps else None
    yf_free = None
    ci = 0
    last_ev = None
    for g in range(G):
        ogl = p.dma("sync", "dog", og[:, 0:KO, :],
                    o_scr[g, 0:KO].rearrange("k p n -> p k n"), deps=_mlist(og_free))
        for cn in range(NCH):
            sl = wo_s.slot(ci)
            p.deps("tensor", [wo_s.use(ci), ogl])
            for cm4 in range(4):
                cm = 4 * cn + cm4
                bank = pb.next()
                p.dep("tensor", bank["free"])
                for k in range(KO):
                    fn = lambda e, k=k, cm4=cm4, sl=sl, b=bank["ap"], wob=wob: e.matmul(
                        b[:, 0:512], wob[:, sl, k, 128 * cm4:128 * (cm4 + 1)],
                        og[:, k, 0:512], start=(k == 0), stop=(k == KO - 1))
                    if k < KO - 1:
                        p.raw("tensor", fn)
                    else:
                        mm = p.inc("tensor", fn)
                p.dep("scalar", mm)
                p.dep("scalar", yf_free)
                if bias_cols is None:
                    evy = p.inc("scalar", lambda e, b=bank["ap"]: e.activation(
                        yf[:, :], b[:, 0:512], AF.Identity))
                else:
                    evy = p.inc("scalar", lambda e, b=bank["ap"], cm=cm: e.activation(
                        yf[:, :], b[:, 0:512], AF.Identity, bias=bias_cols[:, cm:cm + 1]))
                bank["free"] = evy
                tbank = pt.next() if cm4 % 2 == 0 else tbank_keep
                tbank_keep = tbank
                joff = 4 * (cm4 % 2)
                p.deps("tensor", [evy, tbank["free"] if cm4 % 2 == 0 else None])
                for j in range(4):
                    fn = lambda e, j=j, joff=joff, b=tbank["ap"]: e.transpose(
                        b[:, 128 * (joff + j):128 * (joff + j + 1)],
                        yf[:, 128 * j:128 * (j + 1)], idb[:, :])
                    if j < 3:
                        p.raw("tensor", fn)
                    else:
                        tr = p.inc("tensor", fn)
                yf_free = tr
                if cm4 % 2 == 1:
                    cm0 = cm - 1
                    p.dep("vector", tr)
                    ev = p.inc("vector", lambda e, g=g, cm0=cm0, b=tbank["ap"]: e.tensor_copy(
                        ybuf[:, 4 * g:4 * g + 4, 128 * cm0:128 * (cm0 + 2)]
                        .rearrange("p j (cc q) -> p cc j q", cc=2),
                        b[:, 0:1024].rearrange("p (cc j q) -> p cc j q", cc=2, j=4)))
                    tbank["free"] = ev
                    last_ev = ev
            wo_s.done(ci, p.mark("tensor"))
            ci += 1
        og_free = p.mark("tensor")
    return last_ev, og_free



def emit_attn_l1(p, cfg, bufs, pools, io, ln_dep, winit=None, guard=None):
    T, C = cfg["T"], cfg["C"]
    TT, CT = T // 128, C // 128
    NH = C // 128
    G = T // 512
    TC = T // 512
    SC = 1.0 / np.sqrt(128.0)
    lf = bufs["ln_feat"]
    qb, kb, vb = bufs["qbuf1"], bufs["kbuf1"], bufs["vbuf1"]
    tb, ab, a2, at = bufs["tbuf"], bufs["abuf"], bufs["a2buf"], bufs["atbuf"]
    osp = bufs["osp"]
    mneg, st, idb = bufs["mneg"], bufs["stat"], bufs["idb"]
    battn, bvb = bufs["battn"], bufs["bvb"]
    pb, pt, sq = pools["pb"], pools["pt"], pools["sq"]
    W = io["W_attn"]
    b_attn = io["b_attn"]

    p.dep("scalar", winit)
    p.dep("scalar", guard)
    p.dep("gpsimd", winit)
    p.dep("gpsimd", guard)
    wi = [winit] if winit else None
    wq_s = WStream(p, bufs["wq1b"], 1,
                   [(wslice_ap(W, C, 128 * h, 128 * (h + 1)), CT, 128) for h in range(NH)],
                   base="dwq", init_free=wi)
    wk_s = WStream(p, bufs["wk1b"], 1,
                   [(wslice_ap(W, C, C + 128 * h, C + 128 * (h + 1)), CT, 128) for h in range(NH)],
                   base="dwk", init_free=wi)
    wv_s = WStream(p, bufs["wv1b"], 1,
                   [(wslice_ap(W, C, 2 * C + 128 * h, 2 * C + 128 * (h + 1)), CT, 128)
                    for h in range(NH)], base="dwv", init_free=wi)

    NPAR = 4
    SKEW = 3
    qkv_war = None
    ab_free = [None] * NPAR
    a2_free = [None] * NPAR
    at_free = None
    bvb_free = None
    osp_free = [None, None]
    for h in range(NH):
        bvl = p.dma("sync", "dbv", bvb[:, :],
                    b_attn[2 * C + 128 * h:2 * C + 128 * (h + 1)]
                    .unsqueeze(0).partition_broadcast(128),
                    deps=[bvb_free if bvb_free else guard])

        def proj_fm(w_s, dst, bias_col, war):
            sl = w_s.slot(h)
            wb = w_s.wb
            p.dep("tensor", w_s.use(h))
            ev = None
            for tc in range(TC):
                p.dep("tensor", ln_dep[4 * tc + 3])
                bank = pb.next()
                p.dep("tensor", bank["free"])
                for k in range(CT):
                    fn = lambda e, k=k, tc=tc, wb=wb, sl=sl, b=bank["ap"]: e.matmul(
                        b[:, 0:512], wb[:, sl, k, 0:128],
                        lf[:, k, 512 * tc:512 * (tc + 1)],
                        start=(k == 0), stop=(k == CT - 1))
                    if k < CT - 1:
                        p.raw("tensor", fn)
                    else:
                        mm = p.inc("tensor", fn)
                p.dep("scalar", mm)
                p.dep("scalar", war)
                ev = p.inc("scalar", lambda e, tc=tc, dst=dst, bias_col=bias_col,
                           b=bank["ap"]: e.activation(
                    dst[:, 512 * tc:512 * (tc + 1)], b[:, 0:512], AF.Identity,
                    bias=bias_col))
                bank["free"] = ev
            return ev
        q_ready = proj_fm(wq_s, qb, battn[:, h:h + 1], qkv_war)
        wq_s.done(h, p.mark("tensor"))
        k_ready = proj_fm(wk_s, kb, battn[:, NH + h:NH + h + 1], qkv_war)
        wk_s.done(h, p.mark("tensor"))
        slv = wv_s.slot(h)
        p.dep("tensor", wv_s.use(h))
        v_ev = None
        for m in range(TT):
            bank = pb.next()
            p.deps("tensor", [bank["free"], ln_dep[m]])
            for k in range(CT):
                fn = lambda e, k=k, m=m, slv=slv, b=bank["ap"]: e.matmul(
                    b[:, 0:128], lf[:, k, 128 * m:128 * (m + 1)],
                    wv_s.wb[:, slv, k, 0:128], start=(k == 0), stop=(k == CT - 1))
                if k < CT - 1:
                    p.raw("tensor", fn)
                else:
                    mm = p.inc("tensor", fn)
            p.deps("vector", [mm, bvl, qkv_war])
            v_ev = p.inc("vector", lambda e, m=m, b=bank["ap"]: e.scalar_tensor_tensor(
                vb[:, m, 0:128], b[:, 0:128], 1.0, bvb[:, :], ALU.mult, ALU.add))
            bank["free"] = v_ev
        v_ready = v_ev
        bvb_free = v_ready
        wv_s.done(h, p.mark("tensor"))

        units = [(g, il) for g in range(G) for il in range(4)]
        mmv = {}
        scv = {}
        evv = {}

        def emit_z(g):
            p.dep("vector", at_free)
            for il in range(4):
                for jj in range(4 * g + il + 1, 4 * (g + 1)):
                    p.inc("vector", lambda e, jj=jj, il=il: e.memset(
                        at[:, jj, 128 * il:128 * (il + 1)], 0.0))

        def emit_sc(u):
            g, il = units[u]
            i = 4 * g + il
            K = 128 * (i + 1)
            p.deps("tensor", [sq["free"], q_ready, k_ready])
            for kc in range(cdiv(K, 512)):
                ksz = min(512, K - 512 * kc)
                fn = lambda e, i=i, kc=kc, ksz=ksz: e.matmul(
                    sq["ap"][:, 512 * kc:512 * kc + ksz],
                    qb[:, 128 * i:128 * (i + 1)],
                    kb[:, 512 * kc:512 * kc + ksz], start=True, stop=True)
                if kc == cdiv(K, 512) - 1:
                    mmv[u] = p.inc("tensor", fn)
                else:
                    p.raw("tensor", fn)

        ngv = {}

        def emit_chainA(u):
            g, il = units[u]
            i = 4 * g + il
            K = 128 * (i + 1)
            par = u % NPAR
            tp_ = u % 2
            p.dep("scalar", mmv[u])
            p.drain("scalar")
            ev_s = p.inc("scalar", lambda e, K=K, tp_=tp_: e.activation(
                tb[:, tp_, 0:K], sq["ap"][:, 0:K], AF.Identity, scale=SC))
            sq["free"] = ev_s
            p.dep("vector", ev_s)
            p.raw("vector", lambda e, i=i, tp_=tp_: e.tensor_add(
                tb[:, tp_, 128 * i:128 * (i + 1)],
                tb[:, tp_, 128 * i:128 * (i + 1)], mneg[:, :]))
            p.drain("vector")
            ngv[u] = p.inc("vector", lambda e, K=K, par=par, tp_=tp_: e.tensor_reduce(
                st[:, 128 + par:129 + par], tb[:, tp_, 0:K], AX.X, ALU.max, negate=True))

        def emit_chainB(u):
            g, il = units[u]
            i = 4 * g + il
            K = 128 * (i + 1)
            par = u % NPAR
            tp_ = u % 2
            p.deps("scalar", [ngv[u], ab_free[par]])
            p.drain("scalar")
            ex = p.inc("scalar", lambda e, K=K, par=par, tp_=tp_: e.activation(
                ab[:, par, 0:K], tb[:, tp_, 0:K], AF.Exp, bias=st[:, 128 + par:129 + par],
                accum_out=st[:, 136 + par:137 + par]))
            p.dep("vector", ex)
            p.dep("vector", ab_free[par])
            p.drain("vector")
            rz = p.inc("vector", lambda e, par=par: e.reciprocal(
                st[:, 144 + par:145 + par], st[:, 136 + par:137 + par]))
            p.deps("gpsimd", [rz, a2_free[par]])
            scm = p.inc("gpsimd", lambda e, K=K, par=par: e.tensor_scalar(
                a2[:, par, 0:K], ab[:, par, 0:K], st[:, 144 + par:145 + par], 1.0,
                ALU.mult, ALU.mult))
            ab_free[par] = scm
            scv[u] = scm

        def emit_tr(u):
            g, il = units[u]
            i = 4 * g + il
            par = u % NPAR
            for tg in range(cdiv(i + 1, 8)):
                bank = pt.next()
                n8 = min(8, i + 1 - 8 * tg)
                p.deps("tensor", [scv[u], bank["free"]])
                for j in range(n8):
                    jt = 8 * tg + j
                    fn = lambda e, jt=jt, j=j, par=par, b=bank["ap"]: e.transpose(
                        b[:, 128 * j:128 * (j + 1)], a2[:, par, 128 * jt:128 * (jt + 1)],
                        idb[:, :])
                    if j < n8 - 1:
                        p.raw("tensor", fn)
                    else:
                        tr = p.inc("tensor", fn)
                a2_free[par] = tr
                p.dep("vector", tr)
                ev = p.inc("vector",
                           lambda e, tg=tg, n8=n8, il=il, b=bank["ap"]: e.tensor_copy(
                               at[:, 8 * tg:8 * tg + n8, 128 * il:128 * (il + 1)],
                               b[:, 0:128 * n8].rearrange("p (j q) -> p j q", j=n8)))
                bank["free"] = ev
                evv[u] = ev

        def emit_av(u):
            g, il = units[u]
            KJ = 4 * (g + 1)
            bank = pb.next()
            p.deps("tensor", [bank["free"], evv[u], v_ready])
            for j in range(KJ):
                fn = lambda e, j=j, b=bank["ap"]: e.matmul(
                    b[:, 0:512], vb[:, j, 0:128], at[:, j, 0:512],
                    start=(j == 0), stop=(j == KJ - 1))
                if j < KJ - 1:
                    p.raw("tensor", fn)
                else:
                    mm = p.inc("tensor", fn)
            sp = g % 2
            p.dep("scalar", mm)
            p.dep("scalar", osp_free[sp])
            ev = p.inc("scalar", lambda e, sp=sp, b=bank["ap"]: e.activation(
                osp[:, sp, :], b[:, 0:512], AF.Identity))
            bank["free"] = ev
            stm = p.dma("scalar", f"dsp{sp}", io["o_scr"][g, h, :, :], osp[:, sp, :],
                        deps=[ev])
            osp_free[sp] = stm

        av_pend = []

        def flush_av():
            nonlocal at_free
            while av_pend:
                t = av_pend.pop(0)
                emit_av(t)
                at_free = p.mark("tensor")
                nxt = t + 1
                if nxt < len(units):
                    emit_z(units[nxt][0])

        def retire(t):
            emit_tr(t)
            if units[t][1] == 3:
                av_pend.append(t)

        emit_z(0)
        pend = []
        for u in range(len(units)):
            emit_sc(u)
            flush_av()
            if len(pend) == SKEW:
                retire(pend.pop(0))
            emit_chainA(u)
            if u > 0:
                emit_chainB(u - 1)
            pend.append(u)
        emit_chainB(len(units) - 1)
        for t in pend:
            retire(t)
            flush_av()
        qkv_war = p.mark("tensor")
    return [osp_free[0], osp_free[1], p.mark("tensor"), p.mark("vector"),
            p.mark("scalar"), p.mark("gpsimd")]


def emit_wproj_l1(p, cfg, bufs, pools, io, ybuf, o_dep):
    T, C = cfg["T"], cfg["C"]
    TT, CT = T // 128, C // 128
    NCH = C // 512
    o1, bproj = bufs["o1buf"], bufs["bprojb"]
    pb = pools["pb"]
    wp_s = WStream(p, bufs["wpb"], 2,
                   [(wslice_ap(io["W_proj"], C, 512 * cn, 512 * (cn + 1)), CT, 512)
                    for cn in range(NCH)], base="dwp")
    last = None
    for cn in range(NCH):
        sl = wp_s.slot(cn)
        p.deps("tensor", [wp_s.use(cn), o_dep])
        for m in range(TT):
            bank = pb.next()
            p.dep("tensor", bank["free"])
            for k in range(CT):
                fn = lambda e, k=k, m=m, sl=sl, b=bank["ap"]: e.matmul(
                    b[:, 0:512], o1[:, k, 128 * m:128 * (m + 1)],
                    wp_s.wb[:, sl, k, 0:512], start=(k == 0), stop=(k == CT - 1))
                if k < CT - 1:
                    p.raw("tensor", fn)
                else:
                    mm = p.inc("tensor", fn)
            p.dep("vector", mm)
            last = p.inc("vector", lambda e, m=m, cn=cn, b=bank["ap"]: e.scalar_tensor_tensor(
                ybuf[:, m, 512 * cn:512 * (cn + 1)], b[:, 0:512], 1.0,
                bproj[:, 512 * cn:512 * (cn + 1)], ALU.mult, ALU.add))
            bank["free"] = last
        wp_s.done(cn, p.mark("tensor"))
    return last



def emit_ffn0(p, cfg, bufs, pools, io, ln_dep, ybuf, winit=None, gates=None):
    T, C = cfg["T"], cfg["C"]
    TT = T // 128
    NCH = C // 512
    scaleb, sqj, ssp = bufs["scaleb"], bufs["sqjunk"], bufs["ssparts"]
    ws = WStream(p, bufs["wfb"], 2,
                 [(wslice_ap(io["W"], C, 512 * cn, 512 * (cn + 1)), C // 128, 512)
                  for cn in range(NCH)], init_free=winit)
    sq_free = None
    last = None

    def evac(p, m, ci, bank, mm):
        nonlocal sq_free, last
        p.dep("vector", mm)
        ev = p.inc("vector", lambda e, m=m, ci=ci, b=bank["ap"]: e.scalar_tensor_tensor(
            ybuf[:, m, 512 * ci:512 * (ci + 1)], b[:, 0:512], 1.0,
            scaleb[:, 512 * ci:512 * (ci + 1)], ALU.mult, ALU.mult))
        p.deps("scalar", [ev, sq_free])
        if gates is not None and sq_free is None:
            p.dep("scalar", gates[-1])
        NCH_ = cfg["C"] // 512
        sspf = ssp.rearrange("p a b -> p (a b)")
        sq = p.inc("scalar", lambda e, m=m, ci=ci, NCH_=NCH_, sspf=sspf: e.activation(
            sqj[:, :], ybuf[:, m, 512 * ci:512 * (ci + 1)], AF.Square,
            accum_out=sspf[:, m * NCH_ + ci:m * NCH_ + ci + 1]))
        sq_free = sq
        last = sq
        return ev

    emit_linear_tok(p, cfg, pools, bufs["ln_feat"], ln_dep, ws, [512] * NCH, evac)
    return last


def emit_ffn0_gfactor_all(p, cfg, bufs, ssq_dep):
    TT = cfg["T"] // 128
    NCH = cfg["C"] // 512
    st, ssp = bufs["stat"], bufs["ssparts"]
    R0, R1, R2, R3 = 96, 112, 32, 48
    p.dep("vector", ssq_dep)
    p.drain("vector")
    sv = None
    for m in range(TT):
        sv = p.inc("vector", lambda e, m=m: e.tensor_reduce(
            st[:, R0 + m:R0 + m + 1], ssp[:, m, 0:NCH], AX.X, ALU.add))
    p.dep("scalar", sv)
    sq = p.inc("scalar", lambda e: e.activation(
        st[:, R1:R1 + TT], st[:, R0:R0 + TT], AF.Sqrt))
    p.dep("vector", sq)
    vv = p.inc("vector", lambda e: e.tensor_scalar(
        st[:, R1:R1 + TT], st[:, R1:R1 + TT], 1e-7, None, ALU.max))
    p.dep("scalar", vv)
    th = p.inc("scalar", lambda e: e.activation(
        st[:, R2:R2 + TT], st[:, R1:R1 + TT], AF.Tanh))
    p.dep("vector", th)
    p.raw("vector", lambda e: e.tensor_scalar(
        st[:, R0:R0 + TT], st[:, R1:R1 + TT], 1e-7, None, ALU.add))
    p.drain("vector")
    p.raw("vector", lambda e: e.reciprocal(st[:, R3:R3 + TT], st[:, R0:R0 + TT]))
    p.drain("vector")
    p.raw("vector", lambda e: e.tensor_mul(
        st[:, R0:R0 + TT], st[:, R2:R2 + TT], st[:, R3:R3 + TT]))
    p.drain("vector")
    p.raw("vector", lambda e: e.tensor_mul(
        st[:, R3:R3 + TT], st[:, R0:R0 + TT], st[:, R1:R1 + TT]))
    p.drain("vector")
    p.raw("vector", lambda e: e.tensor_scalar(
        st[:, R3:R3 + TT], st[:, R3:R3 + TT], 1e-7, None, ALU.max))
    p.drain("vector")
    p.raw("vector", lambda e: e.reciprocal(st[:, R1:R1 + TT], st[:, R3:R3 + TT]))
    p.drain("vector")
    p.raw("vector", lambda e: e.tensor_scalar(
        st[:, R1:R1 + TT], st[:, R1:R1 + TT], 0.99, 1.0, ALU.mult, ALU.min))
    p.drain("vector")
    gm = p.inc("vector", lambda e: e.tensor_mul(
        st[:, R3:R3 + TT], st[:, R0:R0 + TT], st[:, R1:R1 + TT]))
    return gm


def emit_ffn0_scale_m(p, cfg, bufs, m, gall, ybuf):
    st = bufs["stat"]
    p.dep("vector", gall)
    gm = p.inc("vector", lambda e, m=m: e.tensor_scalar(
        ybuf[:, m, :], ybuf[:, m, :], st[:, 48 + m:49 + m], None, ALU.mult))
    return gm


def emit_ffn1(p, cfg, bufs, pools, io, ln_dep, ybuf, winit=None):
    T, C = cfg["T"], cfg["C"]
    NCH = C // 512
    biasb = bufs["biasb"]
    ws = WStream(p, bufs["wfb"], 2,
                 [(wslice_ap(io["qWT"], C, 512 * cn, 512 * (cn + 1)), C // 128, 512)
                  for cn in range(NCH)], qeng="sync", base="dwt", init_free=winit)
    last = None

    def evac(p, m, ci, bank, mm):
        nonlocal last
        p.dep("vector", mm)
        ev = p.inc("vector", lambda e, m=m, ci=ci, b=bank["ap"]: e.scalar_tensor_tensor(
            ybuf[:, m, 512 * ci:512 * (ci + 1)], b[:, 0:512], 1.0,
            biasb[:, 512 * ci:512 * (ci + 1)], ALU.mult, ALU.add))
        last = ev
        return ev

    emit_linear_tok(p, cfg, pools, bufs["ln_feat"], ln_dep, ws, [512] * NCH, evac)
    return last


def emit_ternary_prologue(p, cfg, bufs, pools, io):
    C = cfg["C"]
    CT = C // 128
    W, qWT = io["W"], io["qWT"]
    wr, qr, g2 = bufs["wrow"], bufs["qrow"], bufs["g2row"]
    st, idb, qt = bufs["stat"], bufs["idb"], bufs["qtmp"]
    pt = pools["pt"]
    wr_free = None
    qr_free = None
    qt_free = None
    for r in range(CT):
        ld = p.dma("sync", "dwr", wr[:, :], W[128 * r:128 * (r + 1), :], deps=[wr_free])
        p.dep("vector", ld)
        p.drain("vector")
        p.raw("vector", lambda e: e.tensor_reduce(
            st[:, 48:49], wr[:, :], AX.X, ALU.add, apply_absolute_value=True))
        p.drain("vector")
        p.raw("vector", lambda e: e.tensor_scalar(
            st[:, 48:49], st[:, 48:49], 0.7 / C, None, ALU.mult))
        p.drain("vector")
        p.raw("vector", lambda e: e.tensor_scalar(
            st[:, 49:50], st[:, 48:49], -1.0, None, ALU.mult))
        p.drain("vector")
        p.dep("vector", qr_free)
        p.raw("vector", lambda e: e.tensor_scalar(
            qr[:, :], wr[:, :], st[:, 48:49], None, ALU.is_gt))
        p.drain("vector")
        p.raw("vector", lambda e: e.tensor_scalar(
            g2[:, :], wr[:, :], st[:, 49:50], None, ALU.is_lt))
        p.drain("vector")
        qm = p.inc("vector", lambda e: e.tensor_sub(qr[:, :], qr[:, :], g2[:, :]))
        wr_free = qm
        for g in range(cdiv(CT, 8)):
            bank = pt.next()
            n8 = min(8, CT - 8 * g)
            p.deps("tensor", [qm, bank["free"]])
            for j in range(n8):
                c = 8 * g + j
                fn = lambda e, c=c, j=j, b=bank["ap"]: e.transpose(
                    b[:, 128 * j:128 * (j + 1)], qr[:, 128 * c:128 * (c + 1)], idb[:, :])
                if j < n8 - 1:
                    p.raw("tensor", fn)
                else:
                    tr = p.inc("tensor", fn)
            qr_free = tr
            p.deps("vector", [tr, qt_free])
            ev = p.inc("vector", lambda e, n8=n8, b=bank["ap"]: e.tensor_copy(
                qt[:, 0:128 * n8], b[:, 0:128 * n8]))
            bank["free"] = ev
            stm = p.dma("scalar", "dqt",
                        qWT[128 * 8 * g:128 * (8 * g + n8), 128 * r:128 * (r + 1)]
                        .rearrange("(j p) o -> p j o", p=128),
                        qt[:, 0:128 * n8].rearrange("p (j o) -> p j o", j=n8),
                        deps=[ev])
            qt_free = stm
    return qt_free



def light_guard(p):
    marks = [p.mark(e) for e in ("tensor", "vector", "scalar", "gpsimd")]
    for eng in ("tensor", "vector", "scalar", "gpsimd"):
        for m in marks:
            p.dep(eng, m)
    return marks


def phase_barrier(p):
    marks = [(sn, v) for sn, v in p.cnt.items()]
    for eng in ENGS:
        for sn, v in marks:
            if v > 0:
                p.wait(eng, sn, v)
    for eng in ENGS:
        p.dirty[eng] = False


def alloc_common(nc, p, cfg, xo_slots=1, xb_slots=1):
    es = p.es
    C = cfg["C"]
    bufs = {}
    bufs["xbuf"] = es.enter_context(nc.sbuf_tensor("xbuf", [128, xb_slots, C], F32)).ap()
    bufs["xout"] = es.enter_context(nc.sbuf_tensor("xo_sb", [128, xo_slots, C], F32)).ap()
    bufs["ln_tok"] = es.enter_context(nc.sbuf_tensor("ln_tok", [128, 2, C], BF16)).ap()
    bufs["stat"] = es.enter_context(nc.sbuf_tensor("stat", [128, 160], F32)).ap()
    bufs["idb"] = es.enter_context(nc.sbuf_tensor("idb", [128, 128], BF16)).ap()
    bufs["mneg"] = es.enter_context(nc.sbuf_tensor("mnegb", [128, 128], F32)).ap()
    bufs["xb_free"] = [None] * xb_slots
    bufs["xo_free"] = [None] * xo_slots
    bufs["lt_free"] = [None, None]
    return bufs


def alloc_pools(nc, p, nf32=2, squad=True):
    es = p.es
    pb = [es.enter_context(nc.psum_tensor(f"pb{i}", [128, 512], F32)).ap()
          for i in range(nf32)]
    pt = [es.enter_context(nc.psum_tensor(f"pt{i}", [128, 1024], BF16)).ap()
          for i in range(2)]
    pools = {"pb": BankPool(pb), "pt": BankPool(pt)}
    if squad:
        sq = es.enter_context(nc.psum_tensor("sq", [128, 2048], F32)).ap()
        pools["sq"] = {"ap": sq, "free": None}
    return pools


def load_consts(p, bufs, io, with_trig):
    p.dma("sync", "dc", bufs["idb"][:, :], io["ident"])
    p.dma("sync", "dc", bufs["mneg"][:, :], io["mneg"])
    if with_trig:
        p.dma("sync", "dc", bufs["cosb"][:, :], io["cos"])
        p.dma("sync", "dc", bufs["sinb"][:, :], io["sin"])


def const_sync(p):
    m = ("dc", p.cnt.get("dc", 0), "DMA:sync")
    for eng in ("tensor", "vector", "scalar", "gpsimd"):
        p.dep(eng, m)


def _mk_nc(name):
    return bass.Bass("TRN2", target_bir_lowering=False)


def np_constants(T, HD0):
    ident = np.eye(128, dtype=np.float32).astype(ml_dtypes.bfloat16)
    mneg = np.where(np.arange(128)[None, :] > np.arange(128)[:, None],
                    np.float32(-1e30), np.float32(0.0)).astype(np.float32)
    half = HD0 // 2
    inv = 1.0 / (10000.0 ** (np.arange(0, half, dtype=np.float64) * 2 / HD0))
    pos = np.arange(T, dtype=np.float64)
    ang = inv[:, None] * pos[None, :]
    return {"ident": ident, "mneg": mneg,
            "cos": np.cos(ang).astype(np.float32),
            "sin": np.sin(ang).astype(np.float32)}



def build_fused(cfg):
    T, C, E = cfg["T"], cfg["C"], cfg["E"]
    NH, NKV, HD = cfg["NH0"], cfg["NKV0"], cfg["HD0"]
    TT, CT = T // 128, C // 128
    KO0 = NH * HD // 128
    NH1 = C // 128
    G = T // 512
    nc = _mk_nc("FUSED")
    io = {}
    io["xin"] = nc.dram_tensor("xin", [E, T, C], F32, kind="ExternalInput").ap()
    io["Wq"] = nc.dram_tensor("Wq", [C, NH * HD], F32, kind="ExternalInput").ap()
    io["Wk"] = nc.dram_tensor("Wk", [C, NKV * HD], F32, kind="ExternalInput").ap()
    io["Wv"] = nc.dram_tensor("Wv", [C, NKV * HD], F32, kind="ExternalInput").ap()
    io["Wo"] = nc.dram_tensor("Wo", [NH * HD, C], F32, kind="ExternalInput").ap()
    io["Wf0"] = nc.dram_tensor("Wf0", [C, C], F32, kind="ExternalInput").ap()
    io["scale0"] = nc.dram_tensor("scale0", [C], F32, kind="ExternalInput").ap()
    io["W_attn"] = nc.dram_tensor("W_attn", [C, 3 * C], F32, kind="ExternalInput").ap()
    io["b_attn"] = nc.dram_tensor("b_attn", [3 * C], F32, kind="ExternalInput").ap()
    io["W_proj"] = nc.dram_tensor("W_proj", [C, C], F32, kind="ExternalInput").ap()
    io["b_proj"] = nc.dram_tensor("b_proj", [C], F32, kind="ExternalInput").ap()
    io["Wf1"] = nc.dram_tensor("Wf1", [C, C], F32, kind="ExternalInput").ap()
    io["bias1"] = nc.dram_tensor("bias1", [C], F32, kind="ExternalInput").ap()
    io["ident"] = nc.dram_tensor("ident", [128, 128], BF16, kind="ExternalInput").ap()
    io["mneg"] = nc.dram_tensor("mneg", [128, 128], F32, kind="ExternalInput").ap()
    io["cos"] = nc.dram_tensor("cos", [128, T], BF16, kind="ExternalInput").ap()
    io["sin"] = nc.dram_tensor("sin", [128, T], BF16, kind="ExternalInput").ap()
    io["xout"] = nc.dram_tensor("xout", [E, T, C], F32, kind="ExternalOutput").ap()
    io["o_scr"] = nc.dram_tensor("o_scr", [G, KO0, 128, 512], BF16, kind="Internal").ap()
    io["qWT"] = nc.dram_tensor("qWT", [C, C], BF16, kind="Internal").ap()
    io["x_scr"] = nc.dram_tensor("x_scr", [3, T, C], F32,
        kind="ExternalOutput" if cfg.get("dbg_x") else "Internal").ap()

    p = Prog(nc)
    es = p.es
    bufs = alloc_common(nc, p, cfg)
    bufs["battn"] = es.enter_context(nc.sbuf_tensor("battn", [128, 3 * NH1], F32)).ap()
    bufs["bprojc"] = es.enter_context(nc.sbuf_tensor("bprojc", [128, CT], F32)).ap()
    bufs["bc8k"] = es.enter_context(nc.sbuf_tensor("bc8k", [128, C], F32)).ap()
    bufs["cosb"] = es.enter_context(nc.sbuf_tensor("cosb", [128, T], BF16)).ap()
    bufs["sinb"] = es.enter_context(nc.sbuf_tensor("sinb", [128, T], BF16)).ap()

    arA = Arena(nc, es, "arA", max(CT * T * 2, 2 * KO0 * 512 * 2))
    lnf = arA.view(0, BF16, [CT, T])
    bufs["ln_feat"] = lnf
    wob_a0 = arA.view(0, BF16, [2, KO0, 512])
    wob_a1 = arA.view(0, BF16, [2, NH1, 512])

    off = 0
    def balloc(name, dtype, shape):
        nonlocal off
        bsz = 4 if dtype == F32 else 2
        v = (name, off, dtype, shape)
        off += (int(np.prod(shape)) * bsz + 31) // 32 * 32
        return v
    r_attn0 = [balloc("wqb", BF16, [1, CT, HD]), balloc("wkb", BF16, [1, CT, HD]),
               balloc("wvb", BF16, [1, CT, HD]), balloc("qbuf", BF16, [2 * HD // 128, T]),
               balloc("kbuf", BF16, [HD // 128, T]), balloc("vbuf", BF16, [TT, HD]),
               balloc("tbuf", F32, [T]), balloc("abuf", BF16, [2, T]),
               balloc("a2buf", BF16, [2, T]), balloc("atbuf", BF16, [TT, 512]),
               balloc("tmp", BF16, [4, 512]), balloc("osp", BF16, [2, 512])]
    sz0 = off
    off = 0
    r_wo = [balloc("og", BF16, [KO0, 512]), balloc("yf", BF16, [512]),
            balloc("ybuf", BF16, [TT, C])]
    szw = off
    off = 0
    r_attn1 = [balloc("wq1b", BF16, [1, CT, 128]), balloc("wk1b", BF16, [1, CT, 128]),
               balloc("wv1b", BF16, [1, CT, 128]), balloc("qbuf1", BF16, [T]),
               balloc("kbuf1", BF16, [T]), balloc("vbuf1", BF16, [TT, 128]),
               balloc("tbuf1", F32, [2, T]), balloc("abuf1", BF16, [4, T]),
               balloc("a2buf1", BF16, [4, T]), balloc("atbuf1", BF16, [TT, 512]),
               balloc("bvb", F32, [128]), balloc("osp1", BF16, [2, 512])]
    sz1 = off
    off = 0
    r_f = [balloc("wfb", BF16, [2, CT, 512]), balloc("ybuf_f", BF16, [TT, C]),
           balloc("sqjunk", BF16, [512]), balloc("ssparts", F32, [TT, C // 512])]
    szf = off
    off = 0
    r_pro = [balloc("wrow", F32, [C]), balloc("qrow", BF16, [C]),
             balloc("g2row", BF16, [C]), balloc("qtmp", BF16, [1024])]
    szp = off
    arB = Arena(nc, es, "arB", max(sz0, szw, sz1, szf, szp))
    for name, o, dt, sh in r_attn0 + r_wo + r_attn1 + r_f + r_pro:
        bufs[name] = arB.view(o, dt, sh)
    pools = alloc_pools(nc, p)

    p.dma("sync", "dc", bufs["idb"][:, :], io["ident"])
    p.dma("sync", "dc", bufs["mneg"][:, :], io["mneg"])
    p.dma("sync", "dc", bufs["battn"][:, :],
          io["b_attn"].rearrange("(t p) -> p t", p=128), slow=True)
    p.dma("sync", "dc", bufs["bprojc"][:, :],
          io["b_proj"].rearrange("(t p) -> p t", p=128), slow=True)
    p.dma("sync", "dc", bufs["cosb"][:, :], io["cos"])
    p.dma("sync", "dc", bufs["sinb"][:, :], io["sin"])
    const_sync(p)

    p.epoch = "P"
    emit_ternary_prologue(p, cfg, bufs, pools, {"W": io["Wf1"], "qWT": io["qWT"]})
    phase_barrier(p)

    outs = []
    mk_F = None
    vm_F = None
    for e in range(E):
        p.epoch = f"A{e}"
        if e == 0:
            phase_barrier(p)
        else:
            p.dep("scalar", mk_F)
            p.dep("scalar", vm_F)
            p.dep("gpsimd", mk_F)
            p.dep("gpsimd", vm_F)
        ln_dep = emit_ln_to_feat(p, cfg, bufs, pools,
                                 lambda m, e=e: io["xin"][e, 128 * m:128 * (m + 1), :],
                                 lnf, reader_free=mk_F)
        attn_marks = emit_attn_l0(p, cfg, bufs, pools, io, ln_dep, winit=mk_F)
        p.epoch = f"B{e}"
        bufs["wob"] = wob_a0
        y_dep, wo_done = emit_wo_l0(p, cfg, bufs, pools, io, bufs["ybuf"],
                                    first_deps=attn_marks)
        ln_dep, add_B = emit_residual_tail(
            p, cfg, bufs, pools, bufs["ybuf"], y_dep,
            lambda m, e=e: io["xin"][e, 128 * m:128 * (m + 1), :],
            lambda m: io["x_scr"][0, 128 * m:128 * (m + 1), :],
            outs, ln_dst=lnf, ln_reader_free=p.mark("tensor"))
        bm_B = p.mark("scalar")
        p.epoch = f"C{e}"
        msc = p.dma("sync", "dcs", bufs["bc8k"][:, :],
                    io["scale0"].unsqueeze(0).partition_broadcast(128),
                    deps=[wo_done])
        p.dep("vector", msc)
        bufs["scaleb"] = bufs["bc8k"]
        y_dep = emit_ffn0(p, cfg, bufs, pools, {"W": io["Wf0"]}, ln_dep,
                          bufs["ybuf_f"], winit=[wo_done], gates=add_B)
        mk_C = p.mark("tensor")
        gall = emit_ffn0_gfactor_all(p, cfg, bufs, y_dep)
        pre = lambda p_, m, g=gall: emit_ffn0_scale_m(p_, cfg, bufs, m, g,
                                                      bufs["ybuf_f"])
        ln_dep, _ = emit_residual_tail(
            p, cfg, bufs, pools, bufs["ybuf_f"], y_dep,
            lambda m: io["x_scr"][0, 128 * m:128 * (m + 1), :],
            lambda m: io["x_scr"][1, 128 * m:128 * (m + 1), :],
            outs, pre_scale=pre, ln_dst=lnf, ln_reader_free=p.mark("tensor"),
            start_dep=bm_B)
        vm_C = p.mark("vector")
        p.epoch = f"D{e}"
        bufs_a1 = dict(bufs)
        for k_, v_ in (("tbuf", "tbuf1"), ("abuf", "abuf1"), ("a2buf", "a2buf1"),
                       ("atbuf", "atbuf1"), ("osp", "osp1")):
            bufs_a1[k_] = bufs[v_]
        attn_marks = emit_attn_l1(p, cfg, bufs_a1, pools, io, ln_dep,
                                  winit=mk_C, guard=vm_C)
        p.epoch = f"E{e}"
        bufs["wob"] = wob_a1
        bufs["og_save"] = bufs["og"]
        bufs["og"] = arB.view(0, BF16, [NH1, 512])
        y_dep, wo1_done = emit_wo_l0(p, cfg, bufs, pools,
                                     {"Wo": io["W_proj"], "o_scr": io["o_scr"]},
                                     bufs["ybuf"], KO=NH1, bias_cols=bufs["bprojc"],
                                     first_deps=attn_marks)
        bufs["og"] = bufs["og_save"]
        ln_dep, add_E = emit_residual_tail(
            p, cfg, bufs, pools, bufs["ybuf"], y_dep,
            lambda m: io["x_scr"][1, 128 * m:128 * (m + 1), :],
            lambda m: io["x_scr"][2, 128 * m:128 * (m + 1), :],
            outs, ln_dst=lnf, ln_reader_free=p.mark("tensor"))
        bm_E = p.mark("scalar")
        p.epoch = f"F{e}"
        mbi = p.dma("sync", "dcs", bufs["bc8k"][:, :],
                    io["bias1"].unsqueeze(0).partition_broadcast(128),
                    deps=[wo1_done])
        p.dep("vector", mbi)
        bufs["biasb"] = bufs["bc8k"]
        y_dep = emit_ffn1(p, cfg, bufs, pools, {"qWT": io["qWT"]}, ln_dep,
                          bufs["ybuf_f"], winit=[wo1_done])
        mk_F = p.mark("tensor")
        _, add_F = emit_residual_tail(
            p, cfg, bufs, pools, bufs["ybuf_f"], y_dep,
            lambda m: io["x_scr"][2, 128 * m:128 * (m + 1), :],
            lambda m, e=e: io["xout"][e, 128 * m:128 * (m + 1), :],
            outs, start_dep=bm_E)
        vm_F = p.mark("vector")
    p.finish(outs)
    p.build()
    return nc


from concourse.bass_utils import run_bass_kernel_spmd

CFG = {"T": 2048, "C": 2048, "E": 2, "NH0": 16, "NKV0": 8, "HD0": 256,
       "gp_scale": True, "exp_psum": True}
N_CORES = 8
_CACHE = {}


def _program():
    if not _CACHE:
        _CACHE["nc"] = build_fused(CFG)
        cs = np_constants(CFG["T"], CFG["HD0"])
        cs["cos"] = cs["cos"].astype(ml_dtypes.bfloat16)
        cs["sin"] = cs["sin"].astype(ml_dtypes.bfloat16)
        _CACHE["consts"] = cs
    return _CACHE


def kernel(L0_attn_X, L0_attn_Wq, L0_attn_Wk, L0_attn_Wv, L0_attn_Wo,
           L0_ffn_weight, L0_ffn_scale, L1_attn_W_attn, L1_attn_b_attn,
           L1_attn_W_proj, L1_attn_b_proj, L1_ffn_weight, L1_ffn_bias):
    pr = _program()
    cs = pr["consts"]
    E = CFG["E"]
    X = np.ascontiguousarray(L0_attn_X, dtype=np.float32)
    cores = list(range(N_CORES))
    in_maps = [{
        "xin": X[E * c:E * (c + 1)],
        "Wq": L0_attn_Wq, "Wk": L0_attn_Wk, "Wv": L0_attn_Wv, "Wo": L0_attn_Wo,
        "Wf0": L0_ffn_weight, "scale0": L0_ffn_scale,
        "W_attn": L1_attn_W_attn, "b_attn": L1_attn_b_attn,
        "W_proj": L1_attn_W_proj, "b_proj": L1_attn_b_proj,
        "Wf1": L1_ffn_weight, "bias1": L1_ffn_bias,
        "ident": cs["ident"], "mneg": cs["mneg"], "cos": cs["cos"], "sin": cs["sin"],
    } for c in cores]
    r = run_bass_kernel_spmd(pr["nc"], in_maps, core_ids=cores)
    return np.concatenate([np.asarray(r.results[c]["xout"]) for c in cores], axis=0)


# version 18
# speedup vs baseline: 1.1544x; 1.1544x over previous
import numpy as np
import ml_dtypes
import concourse.bass as bass
import concourse.mybir as mybir
from contextlib import ExitStack

F32 = mybir.dt.float32
BF16 = mybir.dt.bfloat16
AF = mybir.ActivationFunctionType
ALU = mybir.AluOpType
AX = mybir.AxisListType

ENGS = ("sync", "vector", "tensor", "scalar", "gpsimd")
ESEM = {"tensor": "s_t", "vector": "s_v", "scalar": "s_a", "gpsimd": "s_g"}


def cdiv(a, b):
    return (a + b - 1) // b


class Prog:
    def __init__(self, nc):
        self.nc = nc
        self.es = ExitStack()
        self.ops = {e: [] for e in ENGS}
        self.sems = {}
        self.cnt = {}
        self.waited = {e: {} for e in ENGS}
        self.dirty = {e: False for e in ENGS}
        self.epoch = ""

    def esem(self, eng):
        return ESEM[eng] + self.epoch

    def sem(self, name):
        if name not in self.sems:
            self.sems[name] = self.es.enter_context(self.nc.semaphore(name))
            self.cnt[name] = 0
        return self.sems[name]

    def raw(self, eng, fn):
        self.ops[eng].append(fn)
        self.dirty[eng] = True

    def drain(self, eng):
        if self.dirty[eng]:
            self.ops[eng].append(lambda e: e.drain())
            self.dirty[eng] = False

    def wait(self, eng, semname, val):
        if val <= 0:
            return
        w = self.waited[eng]
        if w.get(semname, 0) >= val:
            return
        w[semname] = val
        s = self.sem(semname)
        self.ops[eng].append(lambda e, s=s, v=val: e.wait_ge(s, v))

    def dep(self, eng, marker):
        if marker is None:
            return
        semname, val, peng = marker
        if peng == eng:
            self.drain(eng)
        else:
            self.wait(eng, semname, val)

    def deps(self, eng, markers):
        for m in markers:
            self.dep(eng, m)

    def inc(self, eng, fn, n=1, semname=None):
        semname = semname or (ESEM[eng] + self.epoch if eng in ESEM else "s_x")
        s = self.sem(semname)
        self.cnt[semname] += n
        def f(e, fn=fn, s=s, n=n):
            fn(e).then_inc(s, n)
        self.ops[eng].append(f)
        self.dirty[eng] = True
        return (semname, self.cnt[semname], eng)

    def mark(self, eng):
        semname = ESEM[eng] + self.epoch
        self.sem(semname)
        return (semname, self.cnt.get(semname, 0), eng)

    def dma(self, qeng, dsem, out, in_, deps=(), slow=False):
        self.deps(qeng, deps)
        if not hasattr(self, "dsems"):
            self.dsems = {}
        self.dsems[dsem] = qeng
        m = self.inc(qeng, lambda e, o=out, i=in_, sl=slow: e.dma_start(
            out=o, in_=i, allow_slow_non_contiguous=sl), 16, semname=dsem)
        return (m[0], m[1], "DMA:" + qeng)

    def finish(self, markers):
        for m in markers:
            if m is not None:
                self.wait("gpsimd", m[0], m[1])

    def build(self):
        with self.es, self.nc.Block() as block:
            for eng in ENGS:
                ops = self.ops[eng]
                getattr(block, eng)(lambda e, ops=ops: [fn(e) for fn in ops])


class BankPool:
    def __init__(self, banks):
        self.banks = [{"ap": b, "free": None} for b in banks]
        self.i = 0

    def next(self):
        b = self.banks[self.i % len(self.banks)]
        self.i += 1
        return b


class Arena:

    def __init__(self, nc, es, name, nbytes):
        nbytes = (nbytes + 31) // 32 * 32
        self.t = es.enter_context(nc.sbuf_tensor(name, [128, nbytes], mybir.dt.uint8))
        self.nbytes = nbytes

    def view(self, off, dtype, shape):
        bsz = 4 if dtype == F32 else 2
        n = int(np.prod(shape))
        assert off % 32 == 0 and off + n * bsz <= self.nbytes, (off, n, bsz, self.nbytes)
        ap = self.t.ap().bitcast(dtype)[:, off // bsz: off // bsz + n]
        if len(shape) == 2:
            ap = ap.rearrange("p (a b) -> p a b", a=shape[0])
        elif len(shape) == 3:
            ap = ap.rearrange("p (a b c) -> p a b c", a=shape[0], b=shape[1])
        return ap


def wslice_ap(w_hbm, r1, c0, c1):
    return w_hbm[0:r1, c0:c1].rearrange("(k p) n -> p k n", p=128)


class WStream:

    def __init__(self, p, wb, nslots, chunks, qeng="gpsimd", depth=None, base="dw",
                 init_free=None):
        self.p = p
        self.wb = wb
        self.S = nslots
        self.chunks = chunks
        self.base = base
        self.qeng = qeng
        self.free = [list(init_free) if init_free else None] * nslots
        self.loaded = [None] * len(chunks)
        self.n_issued = 0
        depth = nslots if depth is None else depth
        for _ in range(min(depth, len(chunks))):
            self._issue()

    def _issue(self):
        i = self.n_issued
        hbm, kt, nsz = self.chunks[i]
        sl = i % self.S
        dst = self.wb[:, sl, 0:kt, 0:nsz] if self.S > 1 else self.wb[:, 0, 0:kt, 0:nsz]
        self.loaded[i] = self.p.dma(self.qeng, f"{self.base}{sl}", dst, hbm,
                                    deps=_mlist(self.free[sl]))
        self.n_issued += 1

    def slot(self, i):
        return i % self.S

    def use(self, i):
        return self.loaded[i]

    def done(self, i, marker):
        self.free[i % self.S] = marker
        if self.n_issued < len(self.chunks):
            self._issue()



def emit_rstd(p, st, mean_off, var_off, scratch, out_r, out_negmr, eps):
    p.drain("vector")
    mv = p.inc("vector", lambda e: e.tensor_scalar(
        st[:, scratch:scratch + 1], st[:, var_off:var_off + 1], eps, None, ALU.add))
    p.dep("scalar", mv)
    sq = p.inc("scalar", lambda e: e.activation(
        st[:, scratch + 1:scratch + 2], st[:, scratch:scratch + 1], AF.Sqrt))
    p.dep("vector", sq)
    p.raw("vector", lambda e: e.reciprocal(
        st[:, out_r:out_r + 1], st[:, scratch + 1:scratch + 2]))
    p.drain("vector")
    rm = p.inc("vector", lambda e: e.tensor_scalar(
        st[:, out_negmr:out_negmr + 1], st[:, mean_off:mean_off + 1],
        st[:, out_r:out_r + 1], -1.0, ALU.mult, ALU.mult))
    return rm


def emit_ln_tile(p, cfg, bufs, pools, src_ap, src_marker, dst_feat, m, lt_sl,
                 reader_free=None):
    C = cfg["C"]
    CT = C // 128
    NCH = C // 512
    lt, st, idb = bufs["ln_tok"], bufs["stat"], bufs["idb"]
    pt = pools["pt"]
    sb = 64 * (m % 2)
    p.dep("vector", src_marker)
    for c in range(NCH):
        if c:
            p.drain("vector")
        p.raw("vector", lambda e, c=c, sb=sb: e.bn_stats(
            st[:, sb + 6 * c:sb + 6 * c + 6], src_ap[:, 512 * c:512 * (c + 1)]))
    p.drain("vector")
    p.raw("vector", lambda e, sb=sb: e.bn_aggr(
        st[:, sb + 24:sb + 26], st[:, sb:sb + 6 * NCH]))
    rm = emit_rstd(p, st, mean_off=sb + 24, var_off=sb + 25, scratch=sb + 26,
                   out_r=sb + 28, out_negmr=sb + 29, eps=1e-5)
    p.dep("scalar", rm)
    p.dep("scalar", src_marker)
    p.dep("scalar", bufs["lt_free"][lt_sl])
    ap = p.inc("scalar", lambda e, sb=sb: e.activation(
        lt[:, lt_sl, :], src_ap[:, :], AF.Identity,
        bias=st[:, sb + 29:sb + 30], scale=st[:, sb + 28:sb + 29]))
    last = None
    for g in range(cdiv(CT, 8)):
        bank = pools["pt"].next()
        n8 = min(8, CT - 8 * g)
        p.dep("tensor", ap)
        p.dep("tensor", bank["free"])
        for j in range(n8):
            c = 8 * g + j
            fn = lambda e, c=c, j=j, b=bank["ap"]: e.transpose(
                b[:, 128 * j:128 * (j + 1)], lt[:, lt_sl, 128 * c:128 * (c + 1)], idb[:, :])
            if j < n8 - 1:
                p.raw("tensor", fn)
            else:
                tr = p.inc("tensor", fn)
        bufs["lt_free"][lt_sl] = tr
        p.dep("vector", tr)
        if reader_free is not None:
            p.dep("vector", reader_free)
        ev = p.inc("vector", lambda e, g=g, n8=n8, m=m, b=bank["ap"]: e.tensor_copy(
            dst_feat[:, 8 * g:8 * g + n8, 128 * m:128 * (m + 1)],
            b[:, 0:128 * n8].rearrange("p (j q) -> p j q", j=n8)))
        bank["free"] = ev
        last = ev
    return last, ap


def emit_ln_to_feat(p, cfg, bufs, pools, x_src_fn, dst_feat, reader_free=None):
    T = cfg["T"]
    TT = T // 128
    xb = bufs["xbuf"]
    XB = xb.shape[1]
    marks = []
    for m in range(TT):
        sl = m % XB
        dep0 = bufs["xb_free"][sl]
        dep0 = dep0 if isinstance(dep0, list) else ([dep0] if dep0 else [])
        ld = p.dma("sync", f"dx{sl}", xb[:, sl, :], x_src_fn(m), deps=dep0)
        last, ap = emit_ln_tile(p, cfg, bufs, pools, xb[:, sl, :], ld, dst_feat,
                                m, lt_sl=m % 2, reader_free=reader_free)
        bufs["xb_free"][sl] = [ap]
        marks.append(last)
    return marks


def _mlist(x):
    return x if isinstance(x, list) else ([x] if x else [])


def emit_residual_tail(p, cfg, bufs, pools, ybuf, y_dep, x_src_fn, x_dst_fn,
                       out_markers, pre_scale=None, ln_dst=None,
                       ln_reader_free=None, start_dep=None):
    T, C = cfg["T"], cfg["C"]
    TT = T // 128
    NCH = C // 512
    xb, xo, st = bufs["xbuf"], bufs["xout"], bufs["stat"]
    XS = xo.shape[1]
    XB = xb.shape[1]
    ln_last = None
    ln_marks = []
    add_marks = []
    p.dep("vector", start_dep)
    for m in range(TT):
        sl = m % XB
        xsl = m % XS
        sb = 64 * (m % 2)
        ld = p.dma("sync", f"dx{sl}", xb[:, sl, :], x_src_fn(m),
                   deps=_mlist(bufs["xb_free"][sl]))
        p.dep("vector", y_dep)
        ps_m = pre_scale(p, m) if pre_scale is not None else None
        p.dep("vector", ps_m)
        for c in range(NCH):
            if c:
                p.drain("vector")
            p.raw("vector", lambda e, m=m, c=c, sb=sb: e.bn_stats(
                st[:, sb + 6 * c:sb + 6 * c + 6], ybuf[:, m, 512 * c:512 * (c + 1)]))
        p.drain("vector")
        p.raw("vector", lambda e, sb=sb: e.bn_aggr(
            st[:, sb + 24:sb + 26], st[:, sb:sb + 6 * NCH]))
        rm = emit_rstd(p, st, mean_off=sb + 24, var_off=sb + 25, scratch=sb + 26,
                       out_r=sb + 28, out_negmr=sb + 29, eps=1e-5)
        p.dep("vector", rm)
        p.deps("vector", _mlist(bufs["xo_free"][xsl]) + [ld])
        xw = p.inc("vector", lambda e, m=m, sl=sl, xsl=xsl, sb=sb: e.tensor_scalar(
            xo[:, xsl, :], ybuf[:, m, :], st[:, sb + 28:sb + 29],
            st[:, sb + 29:sb + 30], ALU.mult, ALU.add))
        p.deps("gpsimd", [xw, ld])
        add = p.inc("gpsimd", lambda e, sl=sl, xsl=xsl: e.tensor_add(
            xo[:, xsl, :], xo[:, xsl, :], xb[:, sl, :]))
        bufs["xb_free"][sl] = add
        add_marks.append(add)
        stm = p.dma("scalar", f"dxo{xsl}", x_dst_fn(m), xo[:, xsl, :], deps=[add])
        out_markers.append(stm)
        bufs["xo_free"][xsl] = [stm]
        if ln_dst is not None:
            ln_last, ap = emit_ln_tile(p, cfg, bufs, pools, xo[:, xsl, :], add,
                                       ln_dst, m, lt_sl=m % 2,
                                       reader_free=ln_reader_free if m == 0 else None)
            bufs["xo_free"][xsl].append(ap)
            ln_marks.append(ln_last)
    return (ln_marks if ln_dst is not None else None), add_marks



def emit_linear_tok(p, cfg, pools, lhs_feat, lhs_dep, ws, nsizes, evac):
    T, C = cfg["T"], cfg["C"]
    TT, CT = T // 128, C // 128
    pb = pools["pb"]
    per_m = isinstance(lhs_dep, list)
    for ci in range(len(nsizes)):
        nsz = nsizes[ci]
        sl = ws.slot(ci)
        p.dep("tensor", ws.use(ci))
        if not per_m:
            p.dep("tensor", lhs_dep)
        for m in range(TT):
            bank = pb.next()
            p.dep("tensor", bank["free"])
            if per_m:
                p.dep("tensor", lhs_dep[m])
            for k in range(CT):
                wap = ws.wb[:, sl, k, 0:nsz]
                fn = lambda e, k=k, m=m, w=wap, nsz=nsz, b=bank["ap"]: e.matmul(
                    b[:, 0:nsz], lhs_feat[:, k, 128 * m:128 * (m + 1)], w,
                    start=(k == 0), stop=(k == CT - 1))
                if k < CT - 1:
                    p.raw("tensor", fn)
                else:
                    mm = p.inc("tensor", fn)
            bank["free"] = evac(p, m, ci, bank, mm)
        ws.done(ci, p.mark("tensor"))



def emit_attn_l0(p, cfg, bufs, pools, io, ln_dep, winit=None):
    T, C = cfg["T"], cfg["C"]
    TT, CT = T // 128, C // 128
    NKV, HD = cfg["NKV0"], cfg["HD0"]
    G = T // 512
    TC = T // 512
    SC = 1.0 / np.sqrt(HD)
    DKT = HD // 128
    lf = bufs["ln_feat"]
    qb, kb, vb = bufs["qbuf"], bufs["kbuf"], bufs["vbuf"]
    tb, ab, a2, at = bufs["tbuf"], bufs["abuf"], bufs["a2buf"], bufs["atbuf"]
    tmp, osp = bufs["tmp"], bufs["osp"]
    cosb, sinb, mneg, st, idb = bufs["cosb"], bufs["sinb"], bufs["mneg"], bufs["stat"], bufs["idb"]
    pb, pt, sq = pools["pb"], pools["pt"], pools["sq"]
    o_scr = io["o_scr"]
    Wq, Wk, Wv = io["Wq"], io["Wk"], io["Wv"]

    wi = [winit] if winit else None
    wq_s = WStream(p, bufs["wqb"], 1,
                   [(wslice_ap(Wq, C, HD * qh, HD * (qh + 1)), CT, HD) for qh in range(2 * NKV)],
                   base="dwq", init_free=wi)
    wk_s = WStream(p, bufs["wkb"], 1,
                   [(wslice_ap(Wk, C, HD * h, HD * (h + 1)), CT, HD) for h in range(NKV)],
                   base="dwk", init_free=wi)
    wv_s = WStream(p, bufs["wvb"], 1,
                   [(wslice_ap(Wv, C, HD * h, HD * (h + 1)), CT, HD) for h in range(NKV)],
                   base="dwv", init_free=wi)

    SKEW = 3
    qkv_war = None
    ab_free = [None, None]
    a2_free = [None, None]
    at_free = None
    osp_free = [None, None]
    tmp_free = None
    spill_marks = []

    def rope_proj(w_s, ci, dst, dst_base, war_dep):
        nonlocal tmp_free
        sl = w_s.slot(ci)
        wb = bufs["wqb"] if w_s is wq_s else bufs["wkb"]
        p.dep("tensor", w_s.use(ci))
        for tc in range(TC):
            p.dep("tensor", ln_dep[4 * tc + 3])
            bA = pb.next()
            bB = pb.next()
            p.deps("tensor", [bA["free"], bB["free"]])
            for half, bank in ((0, bA), (1, bB)):
                for k in range(CT):
                    fn = lambda e, k=k, half=half, tc=tc, b=bank["ap"], wb=wb, sl=sl: e.matmul(
                        b[:, 0:512], wb[:, sl, k, 128 * half:128 * (half + 1)],
                        lf[:, k, 512 * tc:512 * (tc + 1)],
                        start=(k == 0), stop=(k == CT - 1))
                    if k < CT - 1:
                        p.raw("tensor", fn)
                    else:
                        mm = p.inc("tensor", fn)
            p.dep("vector", mm)
            p.dep("vector", tmp_free)
            srcs = ((bA, cosb), (bB, sinb), (bB, cosb), (bA, sinb))
            for ti, (src, trig) in enumerate(srcs[:3]):
                p.raw("vector", lambda e, ti=ti, s=src["ap"], t=trig, tc=tc: e.tensor_mul(
                    tmp[:, ti, :], s[:, 0:512], t[:, 512 * tc:512 * (tc + 1)]))
            vm = p.inc("vector", lambda e, s=srcs[3][0]["ap"], t=srcs[3][1], tc=tc:
                       e.tensor_mul(tmp[:, 3, :], s[:, 0:512], t[:, 512 * tc:512 * (tc + 1)]))
            bA["free"] = vm
            bB["free"] = vm
            p.dep("gpsimd", vm)
            p.dep("gpsimd", war_dep)
            p.raw("gpsimd", lambda e, tc=tc, dst=dst, db=dst_base: e.tensor_sub(
                dst[:, db, 512 * tc:512 * (tc + 1)], tmp[:, 0, :], tmp[:, 1, :]))
            p.drain("gpsimd")
            gm = p.inc("gpsimd", lambda e, tc=tc, dst=dst, db=dst_base: e.tensor_add(
                dst[:, db + 1, 512 * tc:512 * (tc + 1)], tmp[:, 2, :], tmp[:, 3, :]))
            tmp_free = gm
        return p.mark("gpsimd")

    for h in range(NKV):
        q_ready = []
        for pair in range(2):
            q_ready.append(rope_proj(wq_s, 2 * h + pair, qb, 2 * pair, qkv_war))
            wq_s.done(2 * h + pair, p.mark("tensor"))
        k_ready = rope_proj(wk_s, h, kb, 0, qkv_war)
        wk_s.done(h, p.mark("tensor"))
        slv = wv_s.slot(h)
        p.dep("tensor", wv_s.use(h))
        v_ev = None
        for m in range(TT):
            bank = pb.next()
            p.deps("tensor", [bank["free"], ln_dep[m]])
            wvb = bufs["wvb"]
            for k in range(CT):
                fn = lambda e, k=k, m=m, slv=slv, b=bank["ap"], wvb=wvb: e.matmul(
                    b[:, 0:HD], lf[:, k, 128 * m:128 * (m + 1)],
                    wvb[:, slv, k, 0:HD], start=(k == 0), stop=(k == CT - 1))
                if k < CT - 1:
                    p.raw("tensor", fn)
                else:
                    mm = p.inc("tensor", fn)
            p.dep("scalar", mm)
            p.dep("scalar", qkv_war)
            v_ev = p.inc("scalar", lambda e, m=m, b=bank["ap"]: e.activation(
                vb[:, m, 0:HD], b[:, 0:HD], AF.Identity))
            bank["free"] = v_ev
        v_ready = v_ev
        wv_s.done(h, p.mark("tensor"))

        units = [(ql, g, il) for ql in range(2) for g in range(G)
                 for il in range(4)]
        mmv = {}
        scv = {}
        evv = {}

        def emit_z(g):
            p.dep("vector", at_free)
            for il in range(4):
                for jj in range(4 * g + il + 1, 4 * (g + 1)):
                    p.inc("vector", lambda e, jj=jj, il=il: e.memset(
                        at[:, jj, 128 * il:128 * (il + 1)], 0.0))

        def emit_sc(u):
            ql, g, il = units[u]
            i = 4 * g + il
            K = 128 * (i + 1)
            p.deps("tensor", [sq["free"], q_ready[ql], k_ready])
            for kc in range(cdiv(K, 512)):
                ksz = min(512, K - 512 * kc)
                for d in range(DKT):
                    fn = lambda e, d=d, i=i, kc=kc, ksz=ksz, ql=ql: e.matmul(
                        sq["ap"][:, 512 * kc:512 * kc + ksz],
                        qb[:, DKT * ql + d, 128 * i:128 * (i + 1)],
                        kb[:, d, 512 * kc:512 * kc + ksz],
                        start=(d == 0), stop=(d == DKT - 1))
                    if kc == cdiv(K, 512) - 1 and d == DKT - 1:
                        mmv[u] = p.inc("tensor", fn)
                    else:
                        p.raw("tensor", fn)

        def emit_chain(u):
            ql, g, il = units[u]
            i = 4 * g + il
            K = 128 * (i + 1)
            par = u % 2
            p.dep("scalar", mmv[u])
            p.drain("scalar")
            ev_t = p.inc("scalar", lambda e, K=K: e.activation(
                tb[:, 0:K], sq["ap"][:, 0:K], AF.Tanh, scale=SC / 50.0))
            sq["free"] = ev_t
            p.dep("vector", ev_t)
            mk_ = p.inc("vector", lambda e, i=i: e.tensor_add(
                tb[:, 128 * i:128 * (i + 1)], tb[:, 128 * i:128 * (i + 1)], mneg[:, :]))
            p.dep("scalar", mk_)
            p.dep("scalar", ab_free[par])
            p.drain("scalar")
            ex = p.inc("scalar", lambda e, K=K, par=par: e.activation(
                ab[:, par, 0:K], tb[:, 0:K], AF.Exp, scale=50.0,
                accum_out=st[:, 40 + par:41 + par]))
            p.dep("vector", ex)
            p.dep("vector", ab_free[par])
            p.drain("vector")
            rz = p.inc("vector", lambda e, par=par: e.reciprocal(
                st[:, 42 + par:43 + par], st[:, 40 + par:41 + par]))
            p.deps("gpsimd", [rz, a2_free[par]])
            scm = p.inc("gpsimd", lambda e, K=K, par=par: e.tensor_scalar(
                a2[:, par, 0:K], ab[:, par, 0:K], st[:, 42 + par:43 + par], 1.0,
                ALU.mult, ALU.mult))
            ab_free[par] = scm
            scv[u] = scm

        def emit_tr(u):
            ql, g, il = units[u]
            i = 4 * g + il
            par = u % 2
            for tg in range(cdiv(i + 1, 8)):
                bank = pt.next()
                n8 = min(8, i + 1 - 8 * tg)
                p.deps("tensor", [scv[u], bank["free"]])
                for j in range(n8):
                    jt = 8 * tg + j
                    fn = lambda e, jt=jt, j=j, par=par, b=bank["ap"]: e.transpose(
                        b[:, 128 * j:128 * (j + 1)], a2[:, par, 128 * jt:128 * (jt + 1)],
                        idb[:, :])
                    if j < n8 - 1:
                        p.raw("tensor", fn)
                    else:
                        tr = p.inc("tensor", fn)
                a2_free[par] = tr
                p.dep("vector", tr)
                ev = p.inc("vector",
                           lambda e, tg=tg, n8=n8, il=il, b=bank["ap"]: e.tensor_copy(
                               at[:, 8 * tg:8 * tg + n8, 128 * il:128 * (il + 1)],
                               b[:, 0:128 * n8].rearrange("p (j q) -> p j q", j=n8)))
                bank["free"] = ev
                evv[u] = ev

        def emit_av(u):
            ql, g, il = units[u]
            qh = 2 * h + ql
            KJ = 4 * (g + 1)
            for dm in range(DKT):
                bank = pb.next()
                p.deps("tensor", [bank["free"], evv[u], v_ready])
                for j in range(KJ):
                    fn = lambda e, j=j, dm=dm, b=bank["ap"]: e.matmul(
                        b[:, 0:512], vb[:, j, 128 * dm:128 * (dm + 1)], at[:, j, 0:512],
                        start=(j == 0), stop=(j == KJ - 1))
                    if j < KJ - 1:
                        p.raw("tensor", fn)
                    else:
                        mm = p.inc("tensor", fn)
                p.dep("scalar", mm)
                p.dep("scalar", osp_free[dm])
                ev = p.inc("scalar", lambda e, dm=dm, b=bank["ap"]: e.activation(
                    osp[:, dm, :], b[:, 0:512], AF.Identity))
                bank["free"] = ev
                stm = p.dma("scalar", f"dsp{dm}", o_scr[g, qh * DKT + dm, :, :],
                            osp[:, dm, :], deps=[ev])
                osp_free[dm] = stm

        av_pend = []

        def flush_av():
            nonlocal at_free
            while av_pend:
                t = av_pend.pop(0)
                emit_av(t)
                at_free = p.mark("tensor")
                nxt = t + 1
                if nxt < len(units):
                    emit_z(units[nxt][1])

        def retire(t):
            emit_tr(t)
            if units[t][2] == 3:
                av_pend.append(t)

        emit_z(0)
        pend = []
        for u in range(len(units)):
            emit_sc(u)
            flush_av()
            if len(pend) == 2:
                retire(pend.pop(0))
            emit_chain(u)
            pend.append(u)
        for t in pend:
            retire(t)
            flush_av()
        qkv_war = p.mark("tensor")
    spill_marks = [osp_free[0], osp_free[1], p.mark("tensor"), p.mark("vector"),
                   p.mark("scalar"), p.mark("gpsimd")]
    return spill_marks



def emit_wo_l0(p, cfg, bufs, pools, io, ybuf, KO=None, bias_cols=None,
               first_deps=None):
    T, C = cfg["T"], cfg["C"]
    TT, CT = T // 128, C // 128
    G = T // 512
    KO = (cfg["NH0"] * cfg["HD0"]) // 128 if KO is None else KO
    NCH = C // 512
    Wo = io["Wo"]
    o_scr = io["o_scr"]
    og, yf, idb = bufs["og"], bufs["yf"], bufs["idb"]
    wob = bufs["wob"]
    pb, pt = pools["pb"], pools["pt"]
    wo_s = WStream(p, wob, 2,
                   [(wslice_ap(Wo, 128 * KO, 512 * cn, 512 * (cn + 1)), KO, 512)
                    for _ in range(G) for cn in range(NCH)], base="dwo",
                   init_free=first_deps)
    og_free = list(first_deps) if first_deps else None
    yf_free = None
    ci = 0
    last_ev = None
    for g in range(G):
        ogl = p.dma("sync", "dog", og[:, 0:KO, :],
                    o_scr[g, 0:KO].rearrange("k p n -> p k n"), deps=_mlist(og_free))
        for cn in range(NCH):
            sl = wo_s.slot(ci)
            p.deps("tensor", [wo_s.use(ci), ogl])
            for cm4 in range(4):
                cm = 4 * cn + cm4
                bank = pb.next()
                p.dep("tensor", bank["free"])
                for k in range(KO):
                    fn = lambda e, k=k, cm4=cm4, sl=sl, b=bank["ap"], wob=wob: e.matmul(
                        b[:, 0:512], wob[:, sl, k, 128 * cm4:128 * (cm4 + 1)],
                        og[:, k, 0:512], start=(k == 0), stop=(k == KO - 1))
                    if k < KO - 1:
                        p.raw("tensor", fn)
                    else:
                        mm = p.inc("tensor", fn)
                p.dep("scalar", mm)
                p.dep("scalar", yf_free)
                if bias_cols is None:
                    evy = p.inc("scalar", lambda e, b=bank["ap"]: e.activation(
                        yf[:, :], b[:, 0:512], AF.Identity))
                else:
                    evy = p.inc("scalar", lambda e, b=bank["ap"], cm=cm: e.activation(
                        yf[:, :], b[:, 0:512], AF.Identity, bias=bias_cols[:, cm:cm + 1]))
                bank["free"] = evy
                tbank = pt.next() if cm4 % 2 == 0 else tbank_keep
                tbank_keep = tbank
                joff = 4 * (cm4 % 2)
                p.deps("tensor", [evy, tbank["free"] if cm4 % 2 == 0 else None])
                for j in range(4):
                    fn = lambda e, j=j, joff=joff, b=tbank["ap"]: e.transpose(
                        b[:, 128 * (joff + j):128 * (joff + j + 1)],
                        yf[:, 128 * j:128 * (j + 1)], idb[:, :])
                    if j < 3:
                        p.raw("tensor", fn)
                    else:
                        tr = p.inc("tensor", fn)
                yf_free = tr
                if cm4 % 2 == 1:
                    cm0 = cm - 1
                    p.dep("vector", tr)
                    ev = p.inc("vector", lambda e, g=g, cm0=cm0, b=tbank["ap"]: e.tensor_copy(
                        ybuf[:, 4 * g:4 * g + 4, 128 * cm0:128 * (cm0 + 2)]
                        .rearrange("p j (cc q) -> p cc j q", cc=2),
                        b[:, 0:1024].rearrange("p (cc j q) -> p cc j q", cc=2, j=4)))
                    tbank["free"] = ev
                    last_ev = ev
            wo_s.done(ci, p.mark("tensor"))
            ci += 1
        og_free = p.mark("tensor")
    return last_ev, og_free



def emit_attn_l1(p, cfg, bufs, pools, io, ln_dep, winit=None, guard=None):
    T, C = cfg["T"], cfg["C"]
    TT, CT = T // 128, C // 128
    NH = C // 128
    G = T // 512
    TC = T // 512
    SC = 1.0 / np.sqrt(128.0)
    lf = bufs["ln_feat"]
    qb, kb, vb = bufs["qbuf1"], bufs["kbuf1"], bufs["vbuf1"]
    tb, ab, a2, at = bufs["tbuf"], bufs["abuf"], bufs["a2buf"], bufs["atbuf"]
    osp = bufs["osp"]
    mneg, st, idb = bufs["mneg"], bufs["stat"], bufs["idb"]
    battn, bvb = bufs["battn"], bufs["bvb"]
    pb, pt, sq = pools["pb"], pools["pt"], pools["sq"]
    W = io["W_attn"]
    b_attn = io["b_attn"]

    p.dep("scalar", winit)
    p.dep("scalar", guard)
    p.dep("gpsimd", winit)
    p.dep("gpsimd", guard)
    wi = [winit] if winit else None
    wq_s = WStream(p, bufs["wq1b"], 1,
                   [(wslice_ap(W, C, 128 * h, 128 * (h + 1)), CT, 128) for h in range(NH)],
                   base="dwq", init_free=wi)
    wk_s = WStream(p, bufs["wk1b"], 1,
                   [(wslice_ap(W, C, C + 128 * h, C + 128 * (h + 1)), CT, 128) for h in range(NH)],
                   base="dwk", init_free=wi)
    wv_s = WStream(p, bufs["wv1b"], 1,
                   [(wslice_ap(W, C, 2 * C + 128 * h, 2 * C + 128 * (h + 1)), CT, 128)
                    for h in range(NH)], base="dwv", init_free=wi)

    NPAR = 4
    SKEW = 3
    qkv_war = None
    ab_free = [None] * NPAR
    a2_free = [None] * NPAR
    at_free = None
    bvb_free = None
    osp_free = [None, None]
    for h in range(NH):
        bvl = p.dma("sync", "dbv", bvb[:, :],
                    b_attn[2 * C + 128 * h:2 * C + 128 * (h + 1)]
                    .unsqueeze(0).partition_broadcast(128),
                    deps=[bvb_free if bvb_free else guard])

        def proj_fm(w_s, dst, bias_col, war):
            sl = w_s.slot(h)
            wb = w_s.wb
            p.dep("tensor", w_s.use(h))
            ev = None
            for tc in range(TC):
                p.dep("tensor", ln_dep[4 * tc + 3])
                bank = pb.next()
                p.dep("tensor", bank["free"])
                for k in range(CT):
                    fn = lambda e, k=k, tc=tc, wb=wb, sl=sl, b=bank["ap"]: e.matmul(
                        b[:, 0:512], wb[:, sl, k, 0:128],
                        lf[:, k, 512 * tc:512 * (tc + 1)],
                        start=(k == 0), stop=(k == CT - 1))
                    if k < CT - 1:
                        p.raw("tensor", fn)
                    else:
                        mm = p.inc("tensor", fn)
                p.dep("scalar", mm)
                p.dep("scalar", war)
                ev = p.inc("scalar", lambda e, tc=tc, dst=dst, bias_col=bias_col,
                           b=bank["ap"]: e.activation(
                    dst[:, 512 * tc:512 * (tc + 1)], b[:, 0:512], AF.Identity,
                    bias=bias_col))
                bank["free"] = ev
            return ev
        q_ready = proj_fm(wq_s, qb, battn[:, h:h + 1], qkv_war)
        wq_s.done(h, p.mark("tensor"))
        k_ready = proj_fm(wk_s, kb, battn[:, NH + h:NH + h + 1], qkv_war)
        wk_s.done(h, p.mark("tensor"))
        slv = wv_s.slot(h)
        p.dep("tensor", wv_s.use(h))
        v_ev = None
        for m in range(TT):
            bank = pb.next()
            p.deps("tensor", [bank["free"], ln_dep[m]])
            for k in range(CT):
                fn = lambda e, k=k, m=m, slv=slv, b=bank["ap"]: e.matmul(
                    b[:, 0:128], lf[:, k, 128 * m:128 * (m + 1)],
                    wv_s.wb[:, slv, k, 0:128], start=(k == 0), stop=(k == CT - 1))
                if k < CT - 1:
                    p.raw("tensor", fn)
                else:
                    mm = p.inc("tensor", fn)
            p.deps("vector", [mm, bvl, qkv_war])
            v_ev = p.inc("vector", lambda e, m=m, b=bank["ap"]: e.scalar_tensor_tensor(
                vb[:, m, 0:128], b[:, 0:128], 1.0, bvb[:, :], ALU.mult, ALU.add))
            bank["free"] = v_ev
        v_ready = v_ev
        bvb_free = v_ready
        wv_s.done(h, p.mark("tensor"))

        units = [(g, il) for g in range(G) for il in range(4)]
        mmv = {}
        scv = {}
        evv = {}

        def emit_z(g):
            p.dep("vector", at_free)
            for il in range(4):
                for jj in range(4 * g + il + 1, 4 * (g + 1)):
                    p.inc("vector", lambda e, jj=jj, il=il: e.memset(
                        at[:, jj, 128 * il:128 * (il + 1)], 0.0))

        def emit_sc(u):
            g, il = units[u]
            i = 4 * g + il
            K = 128 * (i + 1)
            p.deps("tensor", [sq["free"], q_ready, k_ready])
            for kc in range(cdiv(K, 512)):
                ksz = min(512, K - 512 * kc)
                fn = lambda e, i=i, kc=kc, ksz=ksz: e.matmul(
                    sq["ap"][:, 512 * kc:512 * kc + ksz],
                    qb[:, 128 * i:128 * (i + 1)],
                    kb[:, 512 * kc:512 * kc + ksz], start=True, stop=True)
                if kc == cdiv(K, 512) - 1:
                    mmv[u] = p.inc("tensor", fn)
                else:
                    p.raw("tensor", fn)

        ngv = {}

        def emit_chainA(u):
            g, il = units[u]
            i = 4 * g + il
            K = 128 * (i + 1)
            par = u % NPAR
            tp_ = u % 2
            p.dep("scalar", mmv[u])
            p.drain("scalar")
            ev_s = p.inc("scalar", lambda e, K=K, tp_=tp_: e.activation(
                tb[:, tp_, 0:K], sq["ap"][:, 0:K], AF.Identity, scale=SC))
            sq["free"] = ev_s
            p.dep("vector", ev_s)
            p.raw("vector", lambda e, i=i, tp_=tp_: e.tensor_add(
                tb[:, tp_, 128 * i:128 * (i + 1)],
                tb[:, tp_, 128 * i:128 * (i + 1)], mneg[:, :]))
            p.drain("vector")
            ngv[u] = p.inc("vector", lambda e, K=K, par=par, tp_=tp_: e.tensor_reduce(
                st[:, 128 + par:129 + par], tb[:, tp_, 0:K], AX.X, ALU.max, negate=True))

        def emit_chainB(u):
            g, il = units[u]
            i = 4 * g + il
            K = 128 * (i + 1)
            par = u % NPAR
            tp_ = u % 2
            p.deps("scalar", [ngv[u], ab_free[par]])
            p.drain("scalar")
            ex = p.inc("scalar", lambda e, K=K, par=par, tp_=tp_: e.activation(
                ab[:, par, 0:K], tb[:, tp_, 0:K], AF.Exp, bias=st[:, 128 + par:129 + par],
                accum_out=st[:, 136 + par:137 + par]))
            p.dep("vector", ex)
            p.dep("vector", ab_free[par])
            p.drain("vector")
            rz = p.inc("vector", lambda e, par=par: e.reciprocal(
                st[:, 144 + par:145 + par], st[:, 136 + par:137 + par]))
            p.deps("gpsimd", [rz, a2_free[par]])
            scm = p.inc("gpsimd", lambda e, K=K, par=par: e.tensor_scalar(
                a2[:, par, 0:K], ab[:, par, 0:K], st[:, 144 + par:145 + par], 1.0,
                ALU.mult, ALU.mult))
            ab_free[par] = scm
            scv[u] = scm

        def emit_tr(u):
            g, il = units[u]
            i = 4 * g + il
            par = u % NPAR
            for tg in range(cdiv(i + 1, 8)):
                bank = pt.next()
                n8 = min(8, i + 1 - 8 * tg)
                p.deps("tensor", [scv[u], bank["free"]])
                for j in range(n8):
                    jt = 8 * tg + j
                    fn = lambda e, jt=jt, j=j, par=par, b=bank["ap"]: e.transpose(
                        b[:, 128 * j:128 * (j + 1)], a2[:, par, 128 * jt:128 * (jt + 1)],
                        idb[:, :])
                    if j < n8 - 1:
                        p.raw("tensor", fn)
                    else:
                        tr = p.inc("tensor", fn)
                a2_free[par] = tr
                p.dep("vector", tr)
                ev = p.inc("vector",
                           lambda e, tg=tg, n8=n8, il=il, b=bank["ap"]: e.tensor_copy(
                               at[:, 8 * tg:8 * tg + n8, 128 * il:128 * (il + 1)],
                               b[:, 0:128 * n8].rearrange("p (j q) -> p j q", j=n8)))
                bank["free"] = ev
                evv[u] = ev

        def emit_av(u):
            g, il = units[u]
            KJ = 4 * (g + 1)
            bank = pb.next()
            p.deps("tensor", [bank["free"], evv[u], v_ready])
            for j in range(KJ):
                fn = lambda e, j=j, b=bank["ap"]: e.matmul(
                    b[:, 0:512], vb[:, j, 0:128], at[:, j, 0:512],
                    start=(j == 0), stop=(j == KJ - 1))
                if j < KJ - 1:
                    p.raw("tensor", fn)
                else:
                    mm = p.inc("tensor", fn)
            sp = g % 2
            p.dep("scalar", mm)
            p.dep("scalar", osp_free[sp])
            ev = p.inc("scalar", lambda e, sp=sp, b=bank["ap"]: e.activation(
                osp[:, sp, :], b[:, 0:512], AF.Identity))
            bank["free"] = ev
            stm = p.dma("scalar", f"dsp{sp}", io["o_scr"][g, h, :, :], osp[:, sp, :],
                        deps=[ev])
            osp_free[sp] = stm

        av_pend = []

        def flush_av():
            nonlocal at_free
            while av_pend:
                t = av_pend.pop(0)
                emit_av(t)
                at_free = p.mark("tensor")
                nxt = t + 1
                if nxt < len(units):
                    emit_z(units[nxt][0])

        def retire(t):
            emit_tr(t)
            if units[t][1] == 3:
                av_pend.append(t)

        emit_z(0)
        pend = []
        for u in range(len(units)):
            emit_sc(u)
            flush_av()
            if len(pend) == SKEW:
                retire(pend.pop(0))
            emit_chainA(u)
            if u > 0:
                emit_chainB(u - 1)
            pend.append(u)
        emit_chainB(len(units) - 1)
        for t in pend:
            retire(t)
            flush_av()
        qkv_war = p.mark("tensor")
    return [osp_free[0], osp_free[1], p.mark("tensor"), p.mark("vector"),
            p.mark("scalar"), p.mark("gpsimd")]


def emit_wproj_l1(p, cfg, bufs, pools, io, ybuf, o_dep):
    T, C = cfg["T"], cfg["C"]
    TT, CT = T // 128, C // 128
    NCH = C // 512
    o1, bproj = bufs["o1buf"], bufs["bprojb"]
    pb = pools["pb"]
    wp_s = WStream(p, bufs["wpb"], 2,
                   [(wslice_ap(io["W_proj"], C, 512 * cn, 512 * (cn + 1)), CT, 512)
                    for cn in range(NCH)], base="dwp")
    last = None
    for cn in range(NCH):
        sl = wp_s.slot(cn)
        p.deps("tensor", [wp_s.use(cn), o_dep])
        for m in range(TT):
            bank = pb.next()
            p.dep("tensor", bank["free"])
            for k in range(CT):
                fn = lambda e, k=k, m=m, sl=sl, b=bank["ap"]: e.matmul(
                    b[:, 0:512], o1[:, k, 128 * m:128 * (m + 1)],
                    wp_s.wb[:, sl, k, 0:512], start=(k == 0), stop=(k == CT - 1))
                if k < CT - 1:
                    p.raw("tensor", fn)
                else:
                    mm = p.inc("tensor", fn)
            p.dep("vector", mm)
            last = p.inc("vector", lambda e, m=m, cn=cn, b=bank["ap"]: e.scalar_tensor_tensor(
                ybuf[:, m, 512 * cn:512 * (cn + 1)], b[:, 0:512], 1.0,
                bproj[:, 512 * cn:512 * (cn + 1)], ALU.mult, ALU.add))
            bank["free"] = last
        wp_s.done(cn, p.mark("tensor"))
    return last



def emit_ffn0(p, cfg, bufs, pools, io, ln_dep, ybuf, winit=None, gates=None):
    T, C = cfg["T"], cfg["C"]
    TT = T // 128
    NCH = C // 512
    scaleb, sqj, ssp = bufs["scaleb"], bufs["sqjunk"], bufs["ssparts"]
    ws = WStream(p, bufs["wfb"], 2,
                 [(wslice_ap(io["W"], C, 512 * cn, 512 * (cn + 1)), C // 128, 512)
                  for cn in range(NCH)], init_free=winit)
    sq_free = None
    last = None

    def evac(p, m, ci, bank, mm):
        nonlocal sq_free, last
        p.dep("vector", mm)
        ev = p.inc("vector", lambda e, m=m, ci=ci, b=bank["ap"]: e.scalar_tensor_tensor(
            ybuf[:, m, 512 * ci:512 * (ci + 1)], b[:, 0:512], 1.0,
            scaleb[:, 512 * ci:512 * (ci + 1)], ALU.mult, ALU.mult))
        p.deps("scalar", [ev, sq_free])
        if gates is not None and sq_free is None:
            p.dep("scalar", gates[-1])
        NCH_ = cfg["C"] // 512
        sspf = ssp.rearrange("p a b -> p (a b)")
        sq = p.inc("scalar", lambda e, m=m, ci=ci, NCH_=NCH_, sspf=sspf: e.activation(
            sqj[:, :], ybuf[:, m, 512 * ci:512 * (ci + 1)], AF.Square,
            accum_out=sspf[:, m * NCH_ + ci:m * NCH_ + ci + 1]))
        sq_free = sq
        last = sq
        return ev

    emit_linear_tok(p, cfg, pools, bufs["ln_feat"], ln_dep, ws, [512] * NCH, evac)
    return last


def emit_ffn0_gfactor_all(p, cfg, bufs, ssq_dep):
    TT = cfg["T"] // 128
    NCH = cfg["C"] // 512
    st, ssp = bufs["stat"], bufs["ssparts"]
    R0, R1, R2, R3 = 96, 112, 32, 48
    p.dep("vector", ssq_dep)
    p.drain("vector")
    sv = None
    for m in range(TT):
        sv = p.inc("vector", lambda e, m=m: e.tensor_reduce(
            st[:, R0 + m:R0 + m + 1], ssp[:, m, 0:NCH], AX.X, ALU.add))
    p.dep("scalar", sv)
    sq = p.inc("scalar", lambda e: e.activation(
        st[:, R1:R1 + TT], st[:, R0:R0 + TT], AF.Sqrt))
    p.dep("vector", sq)
    vv = p.inc("vector", lambda e: e.tensor_scalar(
        st[:, R1:R1 + TT], st[:, R1:R1 + TT], 1e-7, None, ALU.max))
    p.dep("scalar", vv)
    th = p.inc("scalar", lambda e: e.activation(
        st[:, R2:R2 + TT], st[:, R1:R1 + TT], AF.Tanh))
    p.dep("vector", th)
    p.raw("vector", lambda e: e.tensor_scalar(
        st[:, R0:R0 + TT], st[:, R1:R1 + TT], 1e-7, None, ALU.add))
    p.drain("vector")
    p.raw("vector", lambda e: e.reciprocal(st[:, R3:R3 + TT], st[:, R0:R0 + TT]))
    p.drain("vector")
    p.raw("vector", lambda e: e.tensor_mul(
        st[:, R0:R0 + TT], st[:, R2:R2 + TT], st[:, R3:R3 + TT]))
    p.drain("vector")
    p.raw("vector", lambda e: e.tensor_mul(
        st[:, R3:R3 + TT], st[:, R0:R0 + TT], st[:, R1:R1 + TT]))
    p.drain("vector")
    p.raw("vector", lambda e: e.tensor_scalar(
        st[:, R3:R3 + TT], st[:, R3:R3 + TT], 1e-7, None, ALU.max))
    p.drain("vector")
    p.raw("vector", lambda e: e.reciprocal(st[:, R1:R1 + TT], st[:, R3:R3 + TT]))
    p.drain("vector")
    p.raw("vector", lambda e: e.tensor_scalar(
        st[:, R1:R1 + TT], st[:, R1:R1 + TT], 0.99, 1.0, ALU.mult, ALU.min))
    p.drain("vector")
    gm = p.inc("vector", lambda e: e.tensor_mul(
        st[:, R3:R3 + TT], st[:, R0:R0 + TT], st[:, R1:R1 + TT]))
    return gm


def emit_ffn0_scale_m(p, cfg, bufs, m, gall, ybuf):
    st = bufs["stat"]
    p.dep("vector", gall)
    gm = p.inc("vector", lambda e, m=m: e.tensor_scalar(
        ybuf[:, m, :], ybuf[:, m, :], st[:, 48 + m:49 + m], None, ALU.mult))
    return gm


def emit_ffn1(p, cfg, bufs, pools, io, ln_dep, ybuf, winit=None):
    T, C = cfg["T"], cfg["C"]
    NCH = C // 512
    biasb = bufs["biasb"]
    ws = WStream(p, bufs["wfb"], 2,
                 [(wslice_ap(io["qWT"], C, 512 * cn, 512 * (cn + 1)), C // 128, 512)
                  for cn in range(NCH)], qeng="sync", base="dwt", init_free=winit)
    last = None

    def evac(p, m, ci, bank, mm):
        nonlocal last
        p.dep("vector", mm)
        ev = p.inc("vector", lambda e, m=m, ci=ci, b=bank["ap"]: e.scalar_tensor_tensor(
            ybuf[:, m, 512 * ci:512 * (ci + 1)], b[:, 0:512], 1.0,
            biasb[:, 512 * ci:512 * (ci + 1)], ALU.mult, ALU.add))
        last = ev
        return ev

    emit_linear_tok(p, cfg, pools, bufs["ln_feat"], ln_dep, ws, [512] * NCH, evac)
    return last


def emit_ternary_prologue(p, cfg, bufs, pools, io):
    C = cfg["C"]
    CT = C // 128
    W, qWT = io["W"], io["qWT"]
    wr, qr, g2 = bufs["wrow"], bufs["qrow"], bufs["g2row"]
    st, idb, qt = bufs["stat"], bufs["idb"], bufs["qtmp"]
    pt = pools["pt"]
    wr_free = None
    qr_free = None
    qt_free = None
    for r in range(CT):
        ld = p.dma("sync", "dwr", wr[:, :], W[128 * r:128 * (r + 1), :], deps=[wr_free])
        p.dep("vector", ld)
        p.drain("vector")
        p.raw("vector", lambda e: e.tensor_reduce(
            st[:, 48:49], wr[:, :], AX.X, ALU.add, apply_absolute_value=True))
        p.drain("vector")
        p.raw("vector", lambda e: e.tensor_scalar(
            st[:, 48:49], st[:, 48:49], 0.7 / C, None, ALU.mult))
        p.drain("vector")
        p.raw("vector", lambda e: e.tensor_scalar(
            st[:, 49:50], st[:, 48:49], -1.0, None, ALU.mult))
        p.drain("vector")
        p.dep("vector", qr_free)
        p.raw("vector", lambda e: e.tensor_scalar(
            qr[:, :], wr[:, :], st[:, 48:49], None, ALU.is_gt))
        p.drain("vector")
        p.raw("vector", lambda e: e.tensor_scalar(
            g2[:, :], wr[:, :], st[:, 49:50], None, ALU.is_lt))
        p.drain("vector")
        qm = p.inc("vector", lambda e: e.tensor_sub(qr[:, :], qr[:, :], g2[:, :]))
        wr_free = qm
        for g in range(cdiv(CT, 8)):
            bank = pt.next()
            n8 = min(8, CT - 8 * g)
            p.deps("tensor", [qm, bank["free"]])
            for j in range(n8):
                c = 8 * g + j
                fn = lambda e, c=c, j=j, b=bank["ap"]: e.transpose(
                    b[:, 128 * j:128 * (j + 1)], qr[:, 128 * c:128 * (c + 1)], idb[:, :])
                if j < n8 - 1:
                    p.raw("tensor", fn)
                else:
                    tr = p.inc("tensor", fn)
            qr_free = tr
            p.deps("vector", [tr, qt_free])
            ev = p.inc("vector", lambda e, n8=n8, b=bank["ap"]: e.tensor_copy(
                qt[:, 0:128 * n8], b[:, 0:128 * n8]))
            bank["free"] = ev
            stm = p.dma("scalar", "dqt",
                        qWT[128 * 8 * g:128 * (8 * g + n8), 128 * r:128 * (r + 1)]
                        .rearrange("(j p) o -> p j o", p=128),
                        qt[:, 0:128 * n8].rearrange("p (j o) -> p j o", j=n8),
                        deps=[ev])
            qt_free = stm
    return qt_free



def light_guard(p):
    marks = [p.mark(e) for e in ("tensor", "vector", "scalar", "gpsimd")]
    for eng in ("tensor", "vector", "scalar", "gpsimd"):
        for m in marks:
            p.dep(eng, m)
    return marks


def phase_barrier(p):
    marks = [(sn, v) for sn, v in p.cnt.items()]
    for eng in ENGS:
        for sn, v in marks:
            if v > 0:
                p.wait(eng, sn, v)
    for eng in ENGS:
        p.dirty[eng] = False


def alloc_common(nc, p, cfg, xo_slots=1, xb_slots=1):
    es = p.es
    C = cfg["C"]
    bufs = {}
    bufs["xbuf"] = es.enter_context(nc.sbuf_tensor("xbuf", [128, xb_slots, C], F32)).ap()
    bufs["xout"] = es.enter_context(nc.sbuf_tensor("xo_sb", [128, xo_slots, C], F32)).ap()
    bufs["ln_tok"] = es.enter_context(nc.sbuf_tensor("ln_tok", [128, 2, C], BF16)).ap()
    bufs["stat"] = es.enter_context(nc.sbuf_tensor("stat", [128, 160], F32)).ap()
    bufs["idb"] = es.enter_context(nc.sbuf_tensor("idb", [128, 128], BF16)).ap()
    bufs["mneg"] = es.enter_context(nc.sbuf_tensor("mnegb", [128, 128], F32)).ap()
    bufs["xb_free"] = [None] * xb_slots
    bufs["xo_free"] = [None] * xo_slots
    bufs["lt_free"] = [None, None]
    return bufs


def alloc_pools(nc, p, nf32=2, squad=True):
    es = p.es
    pb = [es.enter_context(nc.psum_tensor(f"pb{i}", [128, 512], F32)).ap()
          for i in range(nf32)]
    pt = [es.enter_context(nc.psum_tensor(f"pt{i}", [128, 1024], BF16)).ap()
          for i in range(2)]
    pools = {"pb": BankPool(pb), "pt": BankPool(pt)}
    if squad:
        sq = es.enter_context(nc.psum_tensor("sq", [128, 2048], F32)).ap()
        pools["sq"] = {"ap": sq, "free": None}
    return pools


def load_consts(p, bufs, io, with_trig):
    p.dma("sync", "dc", bufs["idb"][:, :], io["ident"])
    p.dma("sync", "dc", bufs["mneg"][:, :], io["mneg"])
    if with_trig:
        p.dma("sync", "dc", bufs["cosb"][:, :], io["cos"])
        p.dma("sync", "dc", bufs["sinb"][:, :], io["sin"])


def const_sync(p):
    m = ("dc", p.cnt.get("dc", 0), "DMA:sync")
    for eng in ("tensor", "vector", "scalar", "gpsimd"):
        p.dep(eng, m)


def _mk_nc(name):
    return bass.Bass("TRN2", target_bir_lowering=False)


def np_constants(T, HD0):
    ident = np.eye(128, dtype=np.float32).astype(ml_dtypes.bfloat16)
    mneg = np.where(np.arange(128)[None, :] > np.arange(128)[:, None],
                    np.float32(-1e30), np.float32(0.0)).astype(np.float32)
    half = HD0 // 2
    inv = 1.0 / (10000.0 ** (np.arange(0, half, dtype=np.float64) * 2 / HD0))
    pos = np.arange(T, dtype=np.float64)
    ang = inv[:, None] * pos[None, :]
    return {"ident": ident, "mneg": mneg,
            "cos": np.cos(ang).astype(np.float32),
            "sin": np.sin(ang).astype(np.float32)}



def build_fused(cfg):
    T, C, E = cfg["T"], cfg["C"], cfg["E"]
    NH, NKV, HD = cfg["NH0"], cfg["NKV0"], cfg["HD0"]
    TT, CT = T // 128, C // 128
    KO0 = NH * HD // 128
    NH1 = C // 128
    G = T // 512
    nc = _mk_nc("FUSED")
    io = {}
    io["xin"] = nc.dram_tensor("xin", [E, T, C], F32, kind="ExternalInput").ap()
    io["Wq"] = nc.dram_tensor("Wq", [C, NH * HD], F32, kind="ExternalInput").ap()
    io["Wk"] = nc.dram_tensor("Wk", [C, NKV * HD], F32, kind="ExternalInput").ap()
    io["Wv"] = nc.dram_tensor("Wv", [C, NKV * HD], F32, kind="ExternalInput").ap()
    io["Wo"] = nc.dram_tensor("Wo", [NH * HD, C], F32, kind="ExternalInput").ap()
    io["Wf0"] = nc.dram_tensor("Wf0", [C, C], F32, kind="ExternalInput").ap()
    io["scale0"] = nc.dram_tensor("scale0", [C], F32, kind="ExternalInput").ap()
    io["W_attn"] = nc.dram_tensor("W_attn", [C, 3 * C], F32, kind="ExternalInput").ap()
    io["b_attn"] = nc.dram_tensor("b_attn", [3 * C], F32, kind="ExternalInput").ap()
    io["W_proj"] = nc.dram_tensor("W_proj", [C, C], F32, kind="ExternalInput").ap()
    io["b_proj"] = nc.dram_tensor("b_proj", [C], F32, kind="ExternalInput").ap()
    io["Wf1"] = nc.dram_tensor("Wf1", [C, C], F32, kind="ExternalInput").ap()
    io["bias1"] = nc.dram_tensor("bias1", [C], F32, kind="ExternalInput").ap()
    io["ident"] = nc.dram_tensor("ident", [128, 128], BF16, kind="ExternalInput").ap()
    io["mneg"] = nc.dram_tensor("mneg", [128, 128], F32, kind="ExternalInput").ap()
    io["cos"] = nc.dram_tensor("cos", [128, T], BF16, kind="ExternalInput").ap()
    io["sin"] = nc.dram_tensor("sin", [128, T], BF16, kind="ExternalInput").ap()
    io["xout"] = nc.dram_tensor("xout", [E, T, C], F32, kind="ExternalOutput").ap()
    io["o_scr"] = nc.dram_tensor("o_scr", [G, KO0, 128, 512], BF16, kind="Internal").ap()
    io["qWT"] = nc.dram_tensor("qWT", [C, C], BF16, kind="Internal").ap()
    io["x_scr"] = nc.dram_tensor("x_scr", [3, T, C], F32,
        kind="ExternalOutput" if cfg.get("dbg_x") else "Internal").ap()

    p = Prog(nc)
    es = p.es
    bufs = alloc_common(nc, p, cfg)
    bufs["battn"] = es.enter_context(nc.sbuf_tensor("battn", [128, 3 * NH1], F32)).ap()
    bufs["bprojc"] = es.enter_context(nc.sbuf_tensor("bprojc", [128, CT], F32)).ap()
    bufs["bc8k"] = es.enter_context(nc.sbuf_tensor("bc8k", [128, C], F32)).ap()
    bufs["cosb"] = es.enter_context(nc.sbuf_tensor("cosb", [128, T], BF16)).ap()
    bufs["sinb"] = es.enter_context(nc.sbuf_tensor("sinb", [128, T], BF16)).ap()

    arA = Arena(nc, es, "arA", max(CT * T * 2, 2 * KO0 * 512 * 2))
    lnf = arA.view(0, BF16, [CT, T])
    bufs["ln_feat"] = lnf
    wob_a0 = arA.view(0, BF16, [2, KO0, 512])
    wob_a1 = arA.view(0, BF16, [2, NH1, 512])

    off = 0
    def balloc(name, dtype, shape):
        nonlocal off
        bsz = 4 if dtype == F32 else 2
        v = (name, off, dtype, shape)
        off += (int(np.prod(shape)) * bsz + 31) // 32 * 32
        return v
    r_attn0 = [balloc("wqb", BF16, [1, CT, HD]), balloc("wkb", BF16, [1, CT, HD]),
               balloc("wvb", BF16, [1, CT, HD]), balloc("qbuf", BF16, [2 * HD // 128, T]),
               balloc("kbuf", BF16, [HD // 128, T]), balloc("vbuf", BF16, [TT, HD]),
               balloc("tbuf", F32, [T]), balloc("abuf", BF16, [2, T]),
               balloc("a2buf", BF16, [2, T]), balloc("atbuf", BF16, [TT, 512]),
               balloc("tmp", BF16, [4, 512]), balloc("osp", BF16, [2, 512])]
    sz0 = off
    off = 0
    r_wo = [balloc("og", BF16, [KO0, 512]), balloc("yf", BF16, [512]),
            balloc("ybuf", BF16, [TT, C])]
    szw = off
    off = 0
    r_attn1 = [balloc("wq1b", BF16, [1, CT, 128]), balloc("wk1b", BF16, [1, CT, 128]),
               balloc("wv1b", BF16, [1, CT, 128]), balloc("qbuf1", BF16, [T]),
               balloc("kbuf1", BF16, [T]), balloc("vbuf1", BF16, [TT, 128]),
               balloc("tbuf1", F32, [2, T]), balloc("abuf1", BF16, [4, T]),
               balloc("a2buf1", BF16, [4, T]), balloc("atbuf1", BF16, [TT, 512]),
               balloc("bvb", F32, [128]), balloc("osp1", BF16, [2, 512])]
    sz1 = off
    off = 0
    r_f = [balloc("wfb", BF16, [2, CT, 512]), balloc("ybuf_f", BF16, [TT, C]),
           balloc("sqjunk", BF16, [512]), balloc("ssparts", F32, [TT, C // 512])]
    szf = off
    off = 0
    r_pro = [balloc("wrow", F32, [C]), balloc("qrow", BF16, [C]),
             balloc("g2row", BF16, [C]), balloc("qtmp", BF16, [1024])]
    szp = off
    arB = Arena(nc, es, "arB", max(sz0, szw, sz1, szf, szp))
    for name, o, dt, sh in r_attn0 + r_wo + r_attn1 + r_f + r_pro:
        bufs[name] = arB.view(o, dt, sh)
    pools = alloc_pools(nc, p)

    p.dma("sync", "dc", bufs["idb"][:, :], io["ident"])
    p.dma("sync", "dc", bufs["mneg"][:, :], io["mneg"])
    p.dma("sync", "dc", bufs["battn"][:, :],
          io["b_attn"].rearrange("(t p) -> p t", p=128), slow=True)
    p.dma("sync", "dc", bufs["bprojc"][:, :],
          io["b_proj"].rearrange("(t p) -> p t", p=128), slow=True)
    p.dma("sync", "dc", bufs["cosb"][:, :], io["cos"])
    p.dma("sync", "dc", bufs["sinb"][:, :], io["sin"])
    const_sync(p)

    p.epoch = "P"
    emit_ternary_prologue(p, cfg, bufs, pools, {"W": io["Wf1"], "qWT": io["qWT"]})
    phase_barrier(p)

    outs = []
    mk_F = None
    vm_F = None
    for e in range(E):
        p.epoch = f"A{e}"
        if e == 0:
            phase_barrier(p)
        else:
            p.dep("scalar", mk_F)
            p.dep("scalar", vm_F)
            p.dep("gpsimd", mk_F)
            p.dep("gpsimd", vm_F)
        ln_dep = emit_ln_to_feat(p, cfg, bufs, pools,
                                 lambda m, e=e: io["xin"][e, 128 * m:128 * (m + 1), :],
                                 lnf, reader_free=mk_F)
        attn_marks = emit_attn_l0(p, cfg, bufs, pools, io, ln_dep, winit=mk_F)
        p.epoch = f"B{e}"
        bufs["wob"] = wob_a0
        y_dep, wo_done = emit_wo_l0(p, cfg, bufs, pools, io, bufs["ybuf"],
                                    first_deps=attn_marks)
        ln_dep, add_B = emit_residual_tail(
            p, cfg, bufs, pools, bufs["ybuf"], y_dep,
            lambda m, e=e: io["xin"][e, 128 * m:128 * (m + 1), :],
            lambda m: io["x_scr"][0, 128 * m:128 * (m + 1), :],
            outs, ln_dst=lnf, ln_reader_free=p.mark("tensor"))
        bm_B = p.mark("scalar")
        p.epoch = f"C{e}"
        msc = p.dma("sync", "dcs", bufs["bc8k"][:, :],
                    io["scale0"].unsqueeze(0).partition_broadcast(128),
                    deps=[wo_done])
        p.dep("vector", msc)
        bufs["scaleb"] = bufs["bc8k"]
        y_dep = emit_ffn0(p, cfg, bufs, pools, {"W": io["Wf0"]}, ln_dep,
                          bufs["ybuf_f"], winit=[wo_done], gates=add_B)
        mk_C = p.mark("tensor")
        gall = emit_ffn0_gfactor_all(p, cfg, bufs, y_dep)
        pre = lambda p_, m, g=gall: emit_ffn0_scale_m(p_, cfg, bufs, m, g,
                                                      bufs["ybuf_f"])
        ln_dep, _ = emit_residual_tail(
            p, cfg, bufs, pools, bufs["ybuf_f"], y_dep,
            lambda m: io["x_scr"][0, 128 * m:128 * (m + 1), :],
            lambda m: io["x_scr"][1, 128 * m:128 * (m + 1), :],
            outs, pre_scale=pre, ln_dst=lnf, ln_reader_free=p.mark("tensor"),
            start_dep=bm_B)
        vm_C = p.mark("vector")
        p.epoch = f"D{e}"
        bufs_a1 = dict(bufs)
        for k_, v_ in (("tbuf", "tbuf1"), ("abuf", "abuf1"), ("a2buf", "a2buf1"),
                       ("atbuf", "atbuf1"), ("osp", "osp1")):
            bufs_a1[k_] = bufs[v_]
        attn_marks = emit_attn_l1(p, cfg, bufs_a1, pools, io, ln_dep,
                                  winit=mk_C, guard=vm_C)
        p.epoch = f"E{e}"
        bufs["wob"] = wob_a1
        bufs["og_save"] = bufs["og"]
        bufs["og"] = arB.view(0, BF16, [NH1, 512])
        y_dep, wo1_done = emit_wo_l0(p, cfg, bufs, pools,
                                     {"Wo": io["W_proj"], "o_scr": io["o_scr"]},
                                     bufs["ybuf"], KO=NH1, bias_cols=bufs["bprojc"],
                                     first_deps=attn_marks)
        bufs["og"] = bufs["og_save"]
        ln_dep, add_E = emit_residual_tail(
            p, cfg, bufs, pools, bufs["ybuf"], y_dep,
            lambda m: io["x_scr"][1, 128 * m:128 * (m + 1), :],
            lambda m: io["x_scr"][2, 128 * m:128 * (m + 1), :],
            outs, ln_dst=lnf, ln_reader_free=p.mark("tensor"))
        bm_E = p.mark("scalar")
        p.epoch = f"F{e}"
        mbi = p.dma("sync", "dcs", bufs["bc8k"][:, :],
                    io["bias1"].unsqueeze(0).partition_broadcast(128),
                    deps=[wo1_done])
        p.dep("vector", mbi)
        bufs["biasb"] = bufs["bc8k"]
        y_dep = emit_ffn1(p, cfg, bufs, pools, {"qWT": io["qWT"]}, ln_dep,
                          bufs["ybuf_f"], winit=[wo1_done])
        mk_F = p.mark("tensor")
        _, add_F = emit_residual_tail(
            p, cfg, bufs, pools, bufs["ybuf_f"], y_dep,
            lambda m: io["x_scr"][2, 128 * m:128 * (m + 1), :],
            lambda m, e=e: io["xout"][e, 128 * m:128 * (m + 1), :],
            outs, start_dep=bm_E)
        vm_F = p.mark("vector")
    p.finish(outs)
    p.build()
    return nc


from concourse.bass_utils import run_bass_kernel_spmd

CFG = {"T": 2048, "C": 2048, "E": 2, "NH0": 16, "NKV0": 8, "HD0": 256,
       "gp_scale": True, "exp_psum": True}
N_CORES = 8
_CACHE = {}


def _program():
    if not _CACHE:
        _CACHE["nc"] = build_fused(CFG)
        cs = np_constants(CFG["T"], CFG["HD0"])
        cs["cos"] = cs["cos"].astype(ml_dtypes.bfloat16)
        cs["sin"] = cs["sin"].astype(ml_dtypes.bfloat16)
        _CACHE["consts"] = cs
    return _CACHE


def kernel(L0_attn_X, L0_attn_Wq, L0_attn_Wk, L0_attn_Wv, L0_attn_Wo,
           L0_ffn_weight, L0_ffn_scale, L1_attn_W_attn, L1_attn_b_attn,
           L1_attn_W_proj, L1_attn_b_proj, L1_ffn_weight, L1_ffn_bias):
    pr = _program()
    cs = pr["consts"]
    E = CFG["E"]
    X = np.ascontiguousarray(L0_attn_X, dtype=np.float32)
    cores = list(range(N_CORES))
    in_maps = [{
        "xin": X[E * c:E * (c + 1)],
        "Wq": L0_attn_Wq, "Wk": L0_attn_Wk, "Wv": L0_attn_Wv, "Wo": L0_attn_Wo,
        "Wf0": L0_ffn_weight, "scale0": L0_ffn_scale,
        "W_attn": L1_attn_W_attn, "b_attn": L1_attn_b_attn,
        "W_proj": L1_attn_W_proj, "b_proj": L1_attn_b_proj,
        "Wf1": L1_ffn_weight, "bias1": L1_ffn_bias,
        "ident": cs["ident"], "mneg": cs["mneg"], "cos": cs["cos"], "sin": cs["sin"],
    } for c in cores]
    r = run_bass_kernel_spmd(pr["nc"], in_maps, core_ids=cores)
    return np.concatenate([np.asarray(r.results[c]["xout"]) for c in cores], axis=0)
